# version 06
# baseline (speedup 1.0000x reference)
import numpy as np
import ml_dtypes
from contextlib import ExitStack

import concourse.bass as bass
import concourse.mybir as mybir
from contextlib import contextmanager
from concourse.bass_primitives import RegisterHandles
from concourse.expressions import RuntimeValue
from concourse.ordered_engine_set import OrderedEngineSet
from concourse.bass_utils import run_bass_kernel_spmd

F32 = mybir.dt.float32
BF16 = mybir.dt.bfloat16
I32 = mybir.dt.int32
AF = mybir.ActivationFunctionType
ALU = mybir.AluOpType
AX = mybir.AxisListType
ds = bass.ds

H = 4096
T = 2048
II = 11008
CK = 32
CI = 86
QT = 16
KVH = 1024
EPS = 1e-5
NEG = -1.0e30
NGI = 12


class Scratch:

    def __init__(self, eng, nrot=4, nslot=8):
        self.eng = eng
        ename = str(eng.engine.value)
        self.rot = [eng.alloc_register(f"sr_{ename}_{i}") for i in range(nrot)]
        self.slot = [eng.alloc_register(f"so_{ename}_{i}") for i in range(nslot)]
        self.tmp = eng.alloc_register(f"st_{ename}")
        self.loops = [eng.alloc_register(f"lp_{ename}_{i}") for i in range(3)]
        self.depth = 0
        self.i = 0
        self.vcache = {}

    def _lin(self, r, terms, const):
        first = True
        for var, coeff in terms:
            if first:
                if coeff == 1:
                    self.eng.reg_mov(r, var)
                else:
                    self.eng.reg_alu(r, var, coeff, ALU.mult)
                first = False
            else:
                if coeff == 1:
                    self.eng.reg_alu(r, r, var, ALU.add)
                else:
                    self.eng.reg_alu(self.tmp, var, coeff, ALU.mult)
                    self.eng.reg_alu(r, r, self.tmp, ALU.add)
        if const:
            self.eng.reg_alu(r, r, const, ALU.add)
        return r

    def wait(self, sem, terms, const=0):
        terms = [(v, c) for (v, c) in terms if c != 0 and v is not None]
        if not terms:
            if const > 0:
                self.eng.wait_ge(sem, const)
            return
        r = self.rot[self.i]
        self.i = (self.i + 1) % len(self.rot)
        self.eng.wait_ge(sem, self._lin(r, terms, const))

    def val(self, slot, terms, const, lo, hi):
        terms = [(v, c) for (v, c) in terms if c != 0 and v is not None]
        if not terms:
            return const
        r = self._lin(self.slot[slot], terms, const)
        return RuntimeValue(RegisterHandles([r]), min_val=lo, max_val=hi)


@contextmanager
def fori(nc, R, start, end):
    eng = R.eng
    r = R.loops[R.depth]
    R.depth += 1
    eng.reg_mov(r, start)
    lid = nc.next_id()
    ls, le = f"fori{lid}_loop", f"fori{lid}_end"
    eng.br(ls)
    engines = OrderedEngineSet([eng.engine])
    with nc.body(ls, valid_engines=engines):
        yield RuntimeValue(RegisterHandles([r]), min_val=start, max_val=end - 1)
        eng.reg_alu(r, r, 1, ALU.add)
        eng.br_cmp(r, end, ls, le, "IS_LT")
    nc.switch_bb(le)
    R.depth -= 1


def build(dbg=(), upto="full", p2_qts=QT, detect_races=True):
    order = ["p1", "p3a", "p2", "p2b", "full"]
    lvl = order.index(upto)
    nc = bass.Bass("TRN2", target_bir_lowering=False,
                   detect_race_conditions=detect_races)

    def scratch_kind(name):
        return "ExternalOutput" if name in dbg else "Internal"

    x_in = nc.dram_tensor("x", [T, H], F32, kind="ExternalInput").ap()
    wqT = nc.dram_tensor("wqT", [H, H], F32, kind="ExternalInput").ap()
    wkT = nc.dram_tensor("wkT", [H, KVH], F32, kind="ExternalInput").ap()
    wvT = nc.dram_tensor("wvT", [H, KVH], F32, kind="ExternalInput").ap()
    woT = nc.dram_tensor("woT", [H, H], F32, kind="ExternalInput").ap()
    gT = nc.dram_tensor("gT", [CI, H, 128], I32, kind="ExternalInput").ap()
    uT = nc.dram_tensor("uT", [CI, H, 128], I32, kind="ExternalInput").ap()
    dT = nc.dram_tensor("dT", [CI, 128, H], I32, kind="ExternalInput").ap()
    gsc_in = nc.dram_tensor("gsc", [128, CI], F32, kind="ExternalInput").ap()
    usc_in = nc.dram_tensor("usc", [128, CI], F32, kind="ExternalInput").ap()
    dsc_in = nc.dram_tensor("dsc", [128, CK], F32, kind="ExternalInput").ap()
    cos_in = nc.dram_tensor("cosT", [128, T], F32, kind="ExternalInput").ap()
    sin_in = nc.dram_tensor("sinN", [128, T], F32, kind="ExternalInput").ap()
    cosq_in = nc.dram_tensor("cosQ", [128, T], F32, kind="ExternalInput").ap()
    sinq_in = nc.dram_tensor("sinQ", [128, T], F32, kind="ExternalInput").ap()
    id_in = nc.dram_tensor("ident", [128, 128], BF16, kind="ExternalInput").ap()
    mk_in = nc.dram_tensor("maskd", [128, 128], F32, kind="ExternalInput").ap()
    mkb_in = nc.dram_tensor("maskb", [128, 128], BF16, kind="ExternalInput").ap()

    q_sc = nc.dram_tensor("q_sc", [CK, 128, T], BF16, kind=scratch_kind("q_sc")).ap()
    k_sc = nc.dram_tensor("k_sc", [8, 128, T], BF16, kind=scratch_kind("k_sc")).ap()
    v_sc = nc.dram_tensor("v_sc", [8, 128, T], BF16, kind=scratch_kind("v_sc")).ap()
    h_sc = nc.dram_tensor("h_sc", [CI, 128, T], BF16, kind=scratch_kind("h_sc")).ap()
    dt_sc = nc.dram_tensor("dt_sc", [CI, 128, H], BF16, kind=scratch_kind("dt_sc")).ap()
    at_sc = nc.dram_tensor("at_sc", [CK, 128, T], BF16, kind=scratch_kind("at_sc")).ap()
    s_sc = nc.dram_tensor("s_sc", [QT, 128, H], F32, kind=scratch_kind("s_sc")).ap()
    xn_dbg = None
    if "xn" in dbg:
        xn_dbg = nc.dram_tensor("xn_dbg", [CK, 128, T], BF16, kind="ExternalOutput").ap()
    out = nc.dram_tensor("out", [T, H], F32, kind="ExternalOutput").ap()

    SR = {
        "sync": Scratch(nc.sync, nslot=4), "tensor": Scratch(nc.tensor, nslot=2),
        "vector": Scratch(nc.vector, nslot=6), "scalar": Scratch(nc.scalar, nslot=2),
        "gpsimd": Scratch(nc.gpsimd, nslot=8),
    }

    with ExitStack() as top:
        E = top.enter_context
        gsc = E(nc.sbuf_tensor("gscs", [128, CI], F32))
        usc = E(nc.sbuf_tensor("uscs", [128, CI], F32))
        gsus = E(nc.sbuf_tensor("gsus", [128, CI], F32))
        dsc = E(nc.sbuf_tensor("dscs", [128, CK], F32))
        sid = E(nc.sbuf_tensor("sid", [128, 128], BF16))
        smask = E(nc.sbuf_tensor("smask", [128, 128], F32))
        smaskb = E(nc.sbuf_tensor("smaskb", [128, 128], BF16))
        epsc = E(nc.sbuf_tensor("epsc", [128, 1], F32))

        sems = {}
        for name in [
            "PRE", "SQ1", "SX1", "VARS", "RSTD", "NMR",
            "NRM", "TRP", "XNE", "VC", "SC",
            "PMM", "PFREE", "QBF",
            "PE3", "UEV", "HS",
            "KVL", "VTR", "VTE",
            "SMM", "MX", "EXP", "PB", "PT", "PTE", "PV", "ATE", "RC",
            "HRL", "DMM", "SIL", "SCL", "CMB",
            "WOM", "TH", "AT4", "TR4",
        ]:
            sems[name] = E(nc.semaphore(name))
        for name in ["X1", "X2", "WL", "QST", "GUL", "DTL", "DTS", "HST",
                     "QL", "DTB", "ATS", "WOL", "XL3", "OUT"]:
            sems[name + "0"] = E(nc.semaphore(name + "0"))
            sems[name + "1"] = E(nc.semaphore(name + "1"))
        S = lambda n: sems[n]
        S2 = lambda n, p: sems[n + str(p)]

        with ExitStack() as ph:
            e = ph.enter_context
            xn = e(nc.sbuf_tensor("xn", [128, CK * T], BF16))

            with ExitStack() as pa:
                e2 = pa.enter_context
                xt = [e2(nc.sbuf_tensor(f"xt{j}", [128, H], F32)) for j in range(2)]
                junk = e2(nc.sbuf_tensor("junk", [128, 3 * QT], F32))
                sx = e2(nc.sbuf_tensor("sx", [128, QT], F32))
                ssq = e2(nc.sbuf_tensor("ssq", [128, QT], F32))
                stat = e2(nc.sbuf_tensor("stat", [128, 4 * QT], F32))
                xnt = [e2(nc.sbuf_tensor(f"xnt{j}", [128, H], BF16)) for j in range(2)]
                ptr = [e2(nc.psum_tensor(f"ptrA{j}", [128, 1024], BF16)) for j in range(2)]
                mean = stat[:, 0:QT]
                var = stat[:, QT:2 * QT]
                rstd = stat[:, 2 * QT:3 * QT]
                nmr = stat[:, 3 * QT:4 * QT]

                with nc.Block("p1a") as blk:
                    @blk.sync
                    def _(sy):
                        sy.dma_start(out=sid[:], in_=id_in).then_inc(S("PRE"), 16)
                        sy.dma_start(out=smask[:], in_=mk_in).then_inc(S("PRE"), 16)
                        sy.dma_start(out=smaskb[:], in_=mkb_in).then_inc(S("PRE"), 16)
                        sy.dma_start(out=gsc[:], in_=gsc_in).then_inc(S("PRE"), 16)
                        sy.dma_start(out=usc[:], in_=usc_in).then_inc(S("PRE"), 16)
                        sy.dma_start(out=dsc[:], in_=dsc_in).then_inc(S("PRE"), 16)
                        for ti in range(QT):
                            if ti >= 2:
                                sy.wait_ge(S("SQ1"), ti - 1)
                                sy.wait_ge(S("SX1"), ti - 1)
                            sy.dma_start(out=xt[ti % 2][:],
                                         in_=x_in[ti * 128:(ti + 1) * 128, :]).then_inc(S2("X1", ti % 2), 16)
                        for ti in range(QT):
                            sy.wait_ge(S("SQ1"), QT - 1 + ti if ti < 2 else QT)
                            sy.wait_ge(S("SX1"), QT - 1 + ti if ti < 2 else QT)
                            if ti >= 2:
                                sy.wait_ge(S("NRM"), ti - 1)
                            sy.dma_start(out=xt[ti % 2][:],
                                         in_=x_in[ti * 128:(ti + 1) * 128, :]).then_inc(S2("X2", ti % 2), 16)

                    @blk.scalar
                    def _(sc):
                        for ti in range(QT):
                            sc.wait_ge(S2("X1", ti % 2), 16 * (ti // 2 + 1))
                            sc.activation(xnt[ti % 2][:], xt[ti % 2][:], AF.Square,
                                          accum_out=ssq[:, ti:ti + 1]).then_inc(S("SQ1"), 1)
                        sc.wait_ge(S("VARS"), 1)
                        sc.activation(junk[:, 0:QT], var, AF.Ln, bias=epsc[:, 0:1]).then_inc(S("SC"), 1)
                        sc.wait_ge(S("SC"), 1)
                        sc.activation(rstd, junk[:, 0:QT], AF.Exp, scale=-0.5).then_inc(S("RSTD"), 1)
                        for ti in range(QT):
                            sc.wait_ge(S2("X2", ti % 2), 16 * (ti // 2 + 1))
                            if ti == 0:
                                sc.wait_ge(S("NMR"), 1)
                            if ti >= 2:
                                sc.wait_ge(S("TRP"), 4 * (ti - 1))
                            sc.activation(xnt[ti % 2][:], xt[ti % 2][:], AF.Identity,
                                          scale=rstd[:, ti:ti + 1],
                                          bias=nmr[:, ti:ti + 1]).then_inc(S("NRM"), 1)

                    @blk.vector
                    def _(v):
                        v.memset(epsc[:], EPS)
                        for ti in range(QT):
                            v.wait_ge(S2("X1", ti % 2), 16 * (ti // 2 + 1))
                            v.reduce_sum(sx[:, ti:ti + 1], xt[ti % 2][:], axis=AX.X).then_inc(S("SX1"), 1)
                        v.wait_ge(S("SQ1"), QT)
                        v.wait_ge(S("SX1"), QT)
                        v.tensor_scalar_mul(mean, sx[:], 1.0 / H)
                        v.tensor_scalar_mul(var, ssq[:], 1.0 / H).then_inc(S("VC"), 1)
                        v.wait_ge(S("VC"), 1)
                        v.tensor_tensor(junk[:, QT:2 * QT], mean, mean, ALU.mult)
                        v.tensor_scalar_mul(junk[:, 2 * QT:3 * QT], mean, -1.0).then_inc(S("VC"), 1)
                        v.wait_ge(S("VC"), 2)
                        v.tensor_tensor(var, var, junk[:, QT:2 * QT], ALU.subtract).then_inc(S("VARS"), 1)
                        v.wait_ge(S("RSTD"), 1)
                        v.tensor_tensor(nmr, junk[:, 2 * QT:3 * QT], rstd, ALU.mult).then_inc(S("NMR"), 1)
                        for ti in range(QT):
                            for grp in range(4):
                                v.wait_ge(S("TRP"), 4 * ti + grp + 1)
                                v.tensor_copy(
                                    out=xn[:].rearrange("p (c t) -> p c t", t=T)[:, grp * 8:(grp + 1) * 8, ti * 128:(ti + 1) * 128],
                                    in_=ptr[grp % 2][:].rearrange("p (c t) -> p c t", t=128),
                                ).then_inc(S("XNE"), 1)

                    @blk.tensor
                    def _(t):
                        t.wait_ge(S("PRE"), 16 * 6)
                        for ti in range(QT):
                            t.wait_ge(S("NRM"), ti + 1)
                            for grp in range(4):
                                gidx = 4 * ti + grp
                                if gidx >= 2:
                                    t.wait_ge(S("XNE"), gidx - 1)
                                for k in range(8):
                                    c = grp * 8 + k
                                    tr = t.transpose(ptr[grp % 2][:, k * 128:(k + 1) * 128],
                                                     xnt[ti % 2][:, c * 128:(c + 1) * 128], sid[:])
                                tr.then_inc(S("TRP"), 1)

            with ExitStack() as pb:
                e2 = pb.enter_context
                cosT = e2(nc.sbuf_tensor("cosTs", [128, T], F32))
                sinN = e2(nc.sbuf_tensor("sinNs", [128, T], F32))
                cosQ = e2(nc.sbuf_tensor("cosQs", [128, T], F32))
                sinQ = e2(nc.sbuf_tensor("sinQs", [128, T], F32))
                wbuf = [e2(nc.sbuf_tensor(f"wb{j}", [128, H], BF16)) for j in range(2)]
                qtmp = e2(nc.sbuf_tensor("qtmp", [128, T], F32))
                t2 = e2(nc.sbuf_tensor("t2", [128, T], F32))
                qbf = [e2(nc.sbuf_tensor(f"qbf{j}", [128, T], BF16)) for j in range(2)]
                pq = [e2(nc.psum_tensor(f"pq{j}", [128, T], F32)) for j in range(2)]

                sections = [(wqT, CK, q_sc, True), (wkT, 8, k_sc, True), (wvT, 8, v_sc, False)]
                bases = [0, CK, CK + 8]

                def em_wload(g, R, base, cv, par, wsrc, first=False):
                    var, coeff, cst = cv
                    if not first:
                        R.wait(S("PMM"), [(var, coeff)], cst + base - 1)
                    nch = wsrc.shape[1] // 128
                    off = R.val(par, [(var, coeff * 128)], cst * 128, 0, (nch - 1) * 128)
                    g.dma_start(
                        out=wbuf[par][:].rearrange("p (c d) -> p c d", d=128),
                        in_=wsrc.rearrange("(c p) d -> p c d", p=128)[:, :, ds(off, 128)],
                    ).then_inc(S2("WL", par), 16)

                def em_wmm(t, R, base, cv, par, first=False):
                    var, coeff, cst = cv
                    R.wait(S2("WL", par), [(var, 16)], 16 * (base // 2) + 16)
                    if not first:
                        R.wait(S("PFREE"), [(var, coeff)], cst + base - 1)
                    for ck in range(CK):
                        for ns in range(4):
                            mm = t.matmul(
                                pq[par][:, ns * 512:(ns + 1) * 512],
                                wbuf[par][:, ck * 128:(ck + 1) * 128],
                                xn[:, ck * T + ns * 512: ck * T + (ns + 1) * 512],
                                start=(ck == 0), stop=(ck == CK - 1),
                            )
                    mm.then_inc(S("PMM"), 1)

                def em_rope(v, R, base, cv, par, rope, first=False, tabs=None):
                    ct, st = tabs if tabs is not None else (cosT, sinN)
                    var, coeff, cst = cv
                    R.wait(S("PMM"), [(var, coeff)], cst + base + 1)
                    if not first:
                        R.wait(S2("QST", par), [(var, 16)], 16 * (base // 2))
                    if rope:
                        if not (isinstance(cst, int) and var is None and cst + base == 0):
                            R.wait(S("QBF"), [(var, coeff)], cst + base)
                        v.tensor_tensor(qtmp[:], pq[par][:], ct[:], ALU.mult)
                        v.tensor_tensor(t2[0:64, :], pq[par][64:128, :], st[0:64, :], ALU.mult)
                        v.tensor_tensor(t2[64:128, :], pq[par][0:64, :], st[64:128, :],
                                        ALU.mult).then_inc(S("PFREE"), 1)
                        R.wait(S("PFREE"), [(var, coeff)], cst + base + 1)
                        v.tensor_tensor(qbf[par][:], qtmp[:], t2[:], ALU.add).then_inc(S("QBF"), 1)
                    else:
                        v.tensor_copy(out=qbf[par][:], in_=pq[par][:]).then_inc(S("PFREE"), 1)

                def em_store(sc, R, base, cv, par, dst, rope, n):
                    var, coeff, cst = cv
                    R.wait(S("QBF") if rope else S("PFREE"), [(var, coeff)], cst + base + 1)
                    off = R.val(par, [(var, coeff)], cst, 0, n - 1)
                    sc.dma_start(out=dst[ds(off, 1)].rearrange("n p t -> p (n t)"),
                                 in_=qbf[par][:]).then_inc(S2("QST", par), 16)

                with nc.Block("p1b") as blk:
                    @blk.sync
                    def _(sy):
                        sy.wait_ge(S("XNE"), 4 * QT)
                        sy.dma_start(out=cosT[:], in_=cos_in).then_inc(S("PRE"), 16)
                        sy.dma_start(out=sinN[:], in_=sin_in).then_inc(S("PRE"), 16)
                        sy.dma_start(out=cosQ[:], in_=cosq_in).then_inc(S("PRE"), 16)
                        sy.dma_start(out=sinQ[:], in_=sinq_in).then_inc(S("PRE"), 16)

                    @blk.gpsimd
                    def _(g):
                        R = SR["gpsimd"]
                        g.wait_ge(S("XNE"), 4 * QT)
                        em_wload(g, R, 0, (None, 0, 0), 0, wqT, first=True)
                        em_wload(g, R, 0, (None, 0, 1), 1, wqT, first=True)
                        with fori(nc, SR["gpsimd"], 1, CK // 2) as cp:
                            em_wload(g, R, 0, (cp, 2, 0), 0, wqT)
                            em_wload(g, R, 0, (cp, 2, 1), 1, wqT)
                        for si in (1, 2):
                            wsrc, n, _, _ = sections[si]
                            with fori(nc, SR["gpsimd"], 0, n // 2) as cp:
                                em_wload(g, R, bases[si], (cp, 2, 0), 0, wsrc)
                                em_wload(g, R, bases[si], (cp, 2, 1), 1, wsrc)

                    @blk.tensor
                    def _(t):
                        R = SR["tensor"]
                        t.wait_ge(S("XNE"), 4 * QT)
                        em_wmm(t, R, 0, (None, 0, 0), 0, first=True)
                        em_wmm(t, R, 0, (None, 0, 1), 1, first=True)
                        with fori(nc, SR["tensor"], 1, CK // 2) as cp:
                            em_wmm(t, R, 0, (cp, 2, 0), 0)
                            em_wmm(t, R, 0, (cp, 2, 1), 1)
                        for si in (1, 2):
                            n = sections[si][1]
                            with fori(nc, SR["tensor"], 0, n // 2) as cp:
                                em_wmm(t, R, bases[si], (cp, 2, 0), 0)
                                em_wmm(t, R, bases[si], (cp, 2, 1), 1)

                    @blk.vector
                    def _(v):
                        R = SR["vector"]
                        v.wait_ge(S("PRE"), 16 * 10)
                        qt_tabs = (cosQ, sinQ)
                        em_rope(v, R, 0, (None, 0, 0), 0, True, first=True, tabs=qt_tabs)
                        em_rope(v, R, 0, (None, 0, 1), 1, True, first=True, tabs=qt_tabs)
                        with fori(nc, SR["vector"], 1, CK // 2) as cp:
                            em_rope(v, R, 0, (cp, 2, 0), 0, True, tabs=qt_tabs)
                            em_rope(v, R, 0, (cp, 2, 1), 1, True, tabs=qt_tabs)
                        for si in (1, 2):
                            n = sections[si][1]
                            rope = sections[si][3]
                            with fori(nc, SR["vector"], 0, n // 2) as cp:
                                em_rope(v, R, bases[si], (cp, 2, 0), 0, rope)
                                em_rope(v, R, bases[si], (cp, 2, 1), 1, rope)

                    @blk.scalar
                    def _(sc):
                        R = SR["scalar"]
                        em_store(sc, R, 0, (None, 0, 0), 0, q_sc, True, CK)
                        em_store(sc, R, 0, (None, 0, 1), 1, q_sc, True, CK)
                        with fori(nc, SR["scalar"], 1, CK // 2) as cp:
                            em_store(sc, R, 0, (cp, 2, 0), 0, q_sc, True, CK)
                            em_store(sc, R, 0, (cp, 2, 1), 1, q_sc, True, CK)
                        for si in (1, 2):
                            n = sections[si][1]
                            dst = sections[si][2]
                            rope = sections[si][3]
                            with fori(nc, SR["scalar"], 0, n // 2) as cp:
                                em_store(sc, R, bases[si], (cp, 2, 0), 0, dst, rope, n)
                                em_store(sc, R, bases[si], (cp, 2, 1), 1, dst, rope, n)

            with ExitStack() as pc:
              if lvl >= 1:
                e2 = pc.enter_context
                gbuf = [e2(nc.sbuf_tensor(f"gb{j}", [128, H], BF16)) for j in range(2)]
                ubuf = [e2(nc.sbuf_tensor(f"ub{j}", [128, H], BF16)) for j in range(2)]
                uev = e2(nc.sbuf_tensor("uev", [128, T], F32))
                hst = [e2(nc.sbuf_tensor(f"hst{j}", [128, T], BF16)) for j in range(2)]
                dtb = [e2(nc.sbuf_tensor(f"dtb{j}", [128, H], BF16)) for j in range(2)]
                pg = e2(nc.psum_tensor("pg", [128, T], F32))
                pu = e2(nc.psum_tensor("pu", [128, T], F32))

                def em3_load(g, R, cv, par, first=False):
                    var, coeff, cst = cv
                    if not first:
                        R.wait(S("PE3"), [(var, 2 * coeff)], 2 * cst - 2)
                    sb = par * 4
                    off = R.val(sb + 0, [(var, coeff)], cst, 0, CI - 1)
                    g.dma_start(out=gbuf[par][:].rearrange("p (c i) -> p c i", i=128),
                                in_=gT[ds(off, 1)].rearrange("n (c p) i -> p (n c) i", p=128)).then_inc(S2("GUL", par), 16)
                    off = R.val(sb + 1, [(var, coeff)], cst, 0, CI - 1)
                    g.dma_start(out=ubuf[par][:].rearrange("p (c i) -> p c i", i=128),
                                in_=uT[ds(off, 1)].rearrange("n (c p) i -> p (n c) i", p=128)).then_inc(S2("GUL", par), 16)
                    if not first:
                        R.wait(S2("DTS", par), [(var, 16)], 0)
                    off = R.val(sb + 2, [(var, coeff)], cst, 0, CI - 1)
                    g.dma_start(out=dtb[par][:], in_=dT[ds(off, 1)].rearrange("n p h -> p (n h)")).then_inc(S2("DTL", par), 16)

                def em3_mm(t, R, cv, par, first=False):
                    var, coeff, cst = cv
                    R.wait(S2("GUL", par), [(var, 32)], 32)
                    for th in range(2):
                        if not first:
                            R.wait(S("HS"), [(var, 2 * coeff)], 2 * cst + th - 1)
                        for w in range(2):
                            buf = gbuf[par] if w == 0 else ubuf[par]
                            ps = pg if w == 0 else pu
                            for ck in range(CK):
                                for ns in range(2):
                                    mm = t.matmul(
                                        ps[:, th * 1024 + ns * 512: th * 1024 + (ns + 1) * 512],
                                        buf[:, ck * 128:(ck + 1) * 128],
                                        xn[:, ck * T + th * 1024 + ns * 512: ck * T + th * 1024 + (ns + 1) * 512],
                                        start=(ck == 0), stop=(ck == CK - 1),
                                    )
                            if w == 1:
                                mm.then_inc(S("PE3"), 1)

                def em3_uev(sc, R, cv, par):
                    var, coeff, cst = cv
                    for th in range(2):
                        R.wait(S("PE3"), [(var, 2 * coeff)], 2 * cst + th + 1)
                        sc.activation(uev[:, th * 1024:(th + 1) * 1024],
                                      pu[:, th * 1024:(th + 1) * 1024], AF.Copy).then_inc(S("UEV"), 1)

                def em3_h(v, R, cv, par, first=False):
                    var, coeff, cst = cv
                    for th in range(2):
                        R.wait(S("UEV"), [(var, 2 * coeff)], 2 * cst + th + 1)
                        if not first:
                            R.wait(S2("HST", par), [(var, 16)], 0)
                        off = R.val(par, [(var, coeff)], cst, 0, CI - 1)
                        v.scalar_tensor_tensor(
                            hst[par][:, th * 1024:(th + 1) * 1024],
                            pg[:, th * 1024:(th + 1) * 1024],
                            gsus[:, ds(off, 1)],
                            uev[:, th * 1024:(th + 1) * 1024],
                            ALU.mult, ALU.mult,
                        ).then_inc(S("HS"), 1)

                def em3_store(sy, R, cv, par):
                    var, coeff, cst = cv
                    R.wait(S("HS"), [(var, 2 * coeff)], 2 * cst + 2)
                    off = R.val(par * 2, [(var, coeff)], cst, 0, CI - 1)
                    sy.dma_start(out=h_sc[ds(off, 1)].rearrange("n p t -> p (n t)"),
                                 in_=hst[par][:]).then_inc(S2("HST", par), 16)
                    R.wait(S2("DTL", par), [(var, 16)], 16)
                    off = R.val(par * 2 + 1, [(var, coeff)], cst, 0, CI - 1)
                    sy.dma_start(out=dt_sc[ds(off, 1)].rearrange("n p h -> p (n h)"),
                                 in_=dtb[par][:]).then_inc(S2("DTS", par), 16)

                with nc.Block("p3a") as blk:
                    @blk.gpsimd
                    def _(g):
                        R = SR["gpsimd"]
                        g.wait_ge(S2("QST", 0), 16 * 24)
                        g.wait_ge(S2("QST", 1), 16 * 24)
                        em3_load(g, R, (None, 0, 0), 0, first=True)
                        em3_load(g, R, (None, 0, 1), 1, first=True)
                        with fori(nc, SR["gpsimd"], 1, CI // 2) as cp:
                            em3_load(g, R, (cp, 2, 0), 0)
                            em3_load(g, R, (cp, 2, 1), 1)

                    @blk.tensor
                    def _(t):
                        R = SR["tensor"]
                        em3_mm(t, R, (None, 0, 0), 0, first=True)
                        em3_mm(t, R, (None, 0, 1), 1)
                        with fori(nc, SR["tensor"], 1, CI // 2) as cp:
                            em3_mm(t, R, (cp, 2, 0), 0)
                            em3_mm(t, R, (cp, 2, 1), 1)

                    @blk.scalar
                    def _(sc):
                        R = SR["scalar"]
                        em3_uev(sc, R, (None, 0, 0), 0)
                        em3_uev(sc, R, (None, 0, 1), 1)
                        with fori(nc, SR["scalar"], 1, CI // 2) as cp:
                            em3_uev(sc, R, (cp, 2, 0), 0)
                            em3_uev(sc, R, (cp, 2, 1), 1)

                    @blk.vector
                    def _(v):
                        R = SR["vector"]
                        v.wait_ge(S("PRE"), 16 * 6)
                        v.tensor_tensor(gsus[:], gsc[:], usc[:], ALU.mult).then_inc(S("VC"), 1)
                        v.wait_ge(S("VC"), 3)
                        em3_h(v, R, (None, 0, 0), 0, first=True)
                        em3_h(v, R, (None, 0, 1), 1, first=True)
                        with fori(nc, SR["vector"], 1, CI // 2) as cp:
                            em3_h(v, R, (cp, 2, 0), 0)
                            em3_h(v, R, (cp, 2, 1), 1)

                    @blk.sync
                    def _(sy):
                        R = SR["sync"]
                        em3_store(sy, R, (None, 0, 0), 0)
                        em3_store(sy, R, (None, 0, 1), 1)
                        with fori(nc, SR["sync"], 1, CI // 2) as cp:
                            em3_store(sy, R, (cp, 2, 0), 0)
                            em3_store(sy, R, (cp, 2, 1), 1)

            if xn_dbg is not None:
                with nc.Block("dbgxn") as blk0:
                    @blk0.scalar
                    def _(sc):
                        sc.wait_ge(S("XNE"), 4 * QT)
                        for c in range(CK):
                            sc.dma_start(out=xn_dbg[c], in_=xn[:, c * T:(c + 1) * T]).then_inc(S("CMB"), 16)
                        sc.wait_ge(S("CMB"), 16 * CK)

        with ExitStack() as ph:
          if lvl >= 2:
            e = ph.enter_context
            k_res = e(nc.sbuf_tensor("k_res", [128, 8 * T], BF16))
            v_tm = e(nc.sbuf_tensor("v_tm", [128, 16 * 1024], BF16))
            v_fm = e(nc.sbuf_tensor("v_fm", [128, 8 * T], BF16))
            q_res = [e(nc.sbuf_tensor(f"q_res{j}", [128, CK * 128], BF16)) for j in range(2)]
            pe_sb = [e(nc.sbuf_tensor(f"pe_sb{j}", [128, T], F32)) for j in range(2)]
            pb = [e(nc.sbuf_tensor(f"pbh{j}", [128, T], BF16)) for j in range(4)]
            pt_sb = [e(nc.sbuf_tensor(f"pt_sb{j}", [128, 512], BF16)) for j in range(2)]
            at_st = [e(nc.sbuf_tensor(f"at_st{j}", [128, 512], BF16)) for j in range(2)]
            nmb = [e(nc.sbuf_tensor(f"nmb{j}", [128, 1], F32)) for j in range(2)]
            lcol = [e(nc.sbuf_tensor(f"lcol{j}", [128, 1], F32)) for j in range(2)]
            rcol = [e(nc.sbuf_tensor(f"rcol{j}", [128, 1], F32)) for j in range(2)]
            p_s = e(nc.psum_tensor("p_s", [128, T], F32))
            p_o = [e(nc.psum_tensor(f"p_o{j}", [128, 512], F32)) for j in range(2)]
            ptr2 = [e(nc.psum_tensor(f"ptr2{j}", [128, 512], BF16)) for j in range(2)]

            at_v = at_sc.rearrange("c p t -> p c t")

            with nc.Block("p2s") as blk:
                @blk.sync
                def _(sy):
                    sy.wait_ge(S("PE3"), 2 * CI)
                    sy.wait_ge(S("HS"), 2 * CI)
                    sy.wait_ge(S2("HST", 0), 16 * 43)
                    sy.wait_ge(S2("HST", 1), 16 * 43)
                    sy.wait_ge(S2("DTS", 0), 16 * 43)
                    sy.wait_ge(S2("DTS", 1), 16 * 43)
                    sy.wait_ge(S2("QST", 0), 16 * 24)
                    sy.wait_ge(S2("QST", 1), 16 * 24)
                    sy.dma_start(out=k_res[:].rearrange("p (c t) -> p c t", t=T),
                                 in_=k_sc.rearrange("c p t -> p c t")).then_inc(S("KVL"), 16)
                    sy.dma_start(out=v_fm[:].rearrange("p (c t) -> p c t", t=T),
                                 in_=v_sc.rearrange("c p t -> p c t")).then_inc(S("KVL"), 16)

                @blk.vector
                def _(v):
                    v.sem_inc(S("ATE"), 2)
                    v.sem_inc(S2("ATS", 0), 16)
                    v.sem_inc(S2("ATS", 1), 16)
                    v.sem_inc(S("EXP"), 2)
                    v.sem_inc(S("PB"), 2)
                    v.sem_inc(S("PTE"), 2)
                    for grp in range(32):
                        v.wait_ge(S("VTR"), grp + 1)
                        kb, cq = grp // 2, grp % 2
                        v.tensor_copy(out=v_tm[:, kb * 1024 + cq * 512: kb * 1024 + (cq + 1) * 512],
                                      in_=ptr2[grp % 2][:]).then_inc(S("VTE"), 1)

                @blk.tensor
                def _(t):
                    t.wait_ge(S("HS"), 2 * CI)
                    t.wait_ge(S("KVL"), 32)
                    for grp in range(32):
                        kb, cq = grp // 2, grp % 2
                        if grp >= 2:
                            t.wait_ge(S("VTE"), grp - 1)
                        for j in range(4):
                            c = cq * 4 + j
                            tr = t.transpose(ptr2[grp % 2][:, j * 128:(j + 1) * 128],
                                             v_fm[:, c * T + kb * 128: c * T + kb * 128 + 128], sid[:])
                        tr.then_inc(S("VTR"), 1)

            for qt in range(min(QT, p2_qts)):
                klen = (qt + 1) * 128
                nfull, rem = divmod(klen, 512)
                heads_base = 32 * qt
                pt_base = 8 * (qt * (qt + 1) // 2)
                hg_base = 8 * qt

                with nc.Block(f"p2q{qt}") as blk:
                    @blk.sync
                    def _(sy):
                        if qt >= 2:
                            sy.wait_ge(S("SMM"), 32 * (qt - 1))
                        sy.dma_start(out=q_res[qt % 2][:].rearrange("p (c t) -> p c t", t=128),
                                     in_=q_sc.rearrange("c p t -> p c t")[:, :, qt * 128:(qt + 1) * 128]
                                     ).then_inc(S2("QL", qt % 2), 16)

                    @blk.tensor
                    def _(t):
                        for hg in range(8):
                            hgp = hg % 2
                            for j in range(4):
                                g = heads_base + hg * 4 + j
                                sb_off = (g % 2) * 1024 if qt <= 7 else 0
                                t.wait_ge(S("EXP"), (g + 1) if qt <= 7 else (g + 2))
                                if j == 0:
                                    t.wait_ge(S2("QL", qt % 2), 16 * (qt // 2 + 1))
                                    t.wait_ge(S("PB"), g + 2)
                                lhs = q_res[qt % 2][:, (hg * 4 + j) * 128:(hg * 4 + j + 1) * 128]
                                for ns in range(nfull):
                                    mm = t.matmul(p_s[:, sb_off + ns * 512: sb_off + (ns + 1) * 512], lhs,
                                                  k_res[:, hg * T + ns * 512: hg * T + (ns + 1) * 512],
                                                  start=True, stop=(qt * 128 < ns * 512 or qt * 128 >= (ns + 1) * 512))
                                if rem:
                                    mm = t.matmul(p_s[:, sb_off + nfull * 512: sb_off + nfull * 512 + rem], lhs,
                                                  k_res[:, hg * T + nfull * 512: hg * T + nfull * 512 + rem],
                                                  start=True, stop=False)
                                mm = t.matmul(p_s[:, sb_off + qt * 128: sb_off + (qt + 1) * 128], sid[:], smaskb[:],
                                              start=False, stop=True)
                                mm.then_inc(S("SMM"), 1)
                            t.wait_ge(S("PB"), heads_base + hg * 4 + 4 + 2)
                            for kb in range(qt + 1):
                                G = pt_base + hg * (qt + 1) + kb
                                if kb < 2:
                                    t.wait_ge(S("PTE"), pt_base + hg * (qt + 1) + 2)
                                else:
                                    t.wait_ge(S("PTE"), G + 1)
                                for j in range(4):
                                    tr = t.transpose(ptr2[kb % 2][:, j * 128:(j + 1) * 128],
                                                     pb[j][:, kb * 128:(kb + 1) * 128], sid[:])
                                tr.then_inc(S("PT"), 1)
                                if kb >= 1:
                                    t.wait_ge(S("PTE"), G + 2)
                                    if kb == 1:
                                        t.wait_ge(S("ATE"), hg_base + hg + 1)
                                    mm = t.matmul(p_o[hgp][:],
                                                  v_tm[:, (kb - 1) * 1024 + hg * 128: (kb - 1) * 1024 + hg * 128 + 128],
                                                  pt_sb[(kb - 1) % 2][:],
                                                  start=(kb == 1), stop=False)
                            t.wait_ge(S("PTE"), pt_base + (hg + 1) * (qt + 1) + 2)
                            if qt == 0:
                                t.wait_ge(S("ATE"), hg_base + hg + 1)
                            mm = t.matmul(p_o[hgp][:],
                                          v_tm[:, qt * 1024 + hg * 128: qt * 1024 + hg * 128 + 128],
                                          pt_sb[qt % 2][:],
                                          start=(qt == 0), stop=True)
                            mm.then_inc(S("PV"), 1)

                    @blk.vector
                    def _(v):
                        R = SR["vector"]
                        with fori(nc, SR["vector"], 0, 4) as hp:
                            for hgp in range(2):
                                def em_norm(j):
                                    gc = heads_base + hgp * 4 + j
                                    R.wait(S("EXP"), [(hp, 8)], gc + 3)
                                    v.reciprocal(rcol[j % 2][:], lcol[j % 2][:]).then_inc(S("RC"), 1)
                                    R.wait(S("RC"), [(hp, 8)], gc + 1)
                                    v.tensor_scalar_mul(pb[j][:, 0:klen], pe_sb[j % 2][:, 0:klen],
                                                        rcol[j % 2][:]).then_inc(S("PB"), 1)
                                for j in range(4):
                                    gc = heads_base + hgp * 4 + j
                                    sb_off = ((hgp * 4 + j) % 2) * 1024 if qt <= 7 else 0
                                    R.wait(S("SMM"), [(hp, 8)], gc + 1)
                                    R.wait(S("EXP"), [(hp, 8)], gc + 1)
                                    v.reduce_max(nmb[j % 2][:], p_s[:, sb_off:sb_off + klen], axis=AX.X,
                                                 negate=True).then_inc(S("MX"), 2)
                                    if j == 1:
                                        R.wait(S("PT"), [(hp, 2 * (qt + 1))], pt_base + hgp * (qt + 1))
                                    if j >= 1:
                                        em_norm(j - 1)
                                em_norm(3)
                                for kb in range(qt + 1):
                                    R.wait(S("PT"), [(hp, 2 * (qt + 1))], pt_base + hgp * (qt + 1) + kb + 1)
                                    v.tensor_copy(out=pt_sb[kb % 2][:], in_=ptr2[kb % 2][:]).then_inc(S("PTE"), 1)
                                R.wait(S("PV"), [(hp, 2)], hg_base + hgp + 1)
                                R.wait(S2("ATS", hgp), [(hp, 16)], 64 * qt + 16)
                                v.tensor_copy(out=at_st[hgp][:], in_=p_o[hgp][:]).then_inc(S("ATE"), 1)

                    @blk.scalar
                    def _(sc):
                        for hg in range(8):
                            hgp = hg % 2
                            for j in range(4):
                                g = heads_base + hg * 4 + j
                                sb_off = (g % 2) * 1024 if qt <= 7 else 0
                                sc.wait_ge(S("MX"), 2 * g + 2)
                                sc.wait_ge(S("PB"), g + 1)
                                sc.activation(pe_sb[j % 2][:, 0:klen], p_s[:, sb_off:sb_off + klen], AF.Exp,
                                              bias=nmb[j % 2][:],
                                              accum_out=lcol[j % 2][:]).then_inc(S("EXP"), 1)
                            sc.wait_ge(S("ATE"), hg_base + hg + 1 + 2)
                            sc.dma_start(
                                out=at_v[:, hg * 4:(hg + 1) * 4, qt * 128:(qt + 1) * 128],
                                in_=at_st[hgp][:].rearrange("p (j q) -> p j q", q=128),
                            ).then_inc(S2("ATS", hgp), 16)

        with ExitStack() as ph:
          if lvl >= 3:
            e = ph.enter_context
            h_res = e(nc.sbuf_tensor("h_res", [128, CI * 512], BF16))
            dtb2 = [e(nc.sbuf_tensor(f"dtb2{j}", [128, 8 * 512], BF16)) for j in range(2)]
            at4 = e(nc.sbuf_tensor("at4", [128, CK * 512], BF16))
            wo_buf = [e(nc.sbuf_tensor(f"wo{j}", [128, 512], BF16)) for j in range(2)]
            stmp = e(nc.sbuf_tensor("stmp", [128, 2048], F32))
            gst = e(nc.sbuf_tensor("gst", [128, 2048], BF16))
            prodb = e(nc.sbuf_tensor("prodb", [128, 2048], BF16))
            xb = [e(nc.sbuf_tensor(f"xb{j}", [128, 512], F32)) for j in range(2)]
            ch_f = [e(nc.sbuf_tensor(f"ch{j}", [128, 512], F32)) for j in range(2)]
            pd = e(nc.psum_tensor("pd", [128, 2048], F32))
            p_wf = e(nc.psum_tensor("p_wf", [128, 1024], F32))
            ptrT = e(nc.psum_tensor("ptrT", [128, 2048], BF16))

            h_v = h_sc.rearrange("c p t -> p c t")
            dt_v = dt_sc.rearrange("c p h -> p c h")
            at_v4 = at_sc.rearrange("c p t -> p c t")
            ATS_ALL = 16 * 64 + 16

            def ig_base(ig):
                return ig * 8 if ig < 10 else 80 + (ig - 10) * 3

            with nc.Block("p4") as blk:
                @blk.sync
                def _(sy):
                    sy.wait_ge(S2("ATS", 0), ATS_ALL)
                    sy.wait_ge(S2("ATS", 1), ATS_ALL)
                    for t4 in range(4):
                        if t4 >= 1:
                            sy.wait_ge(S("DMM"), t4 * 8 * NGI + 2)
                        for piece in range(8):
                            c0 = piece * 11
                            cnt = min(11, CI - c0)
                            if piece >= 1:
                                sy.wait_ge(S("HRL"), 128 * t4 + 16 * piece)
                            sy.dma_start(
                                out=h_res[:].rearrange("p (c t) -> p c t", t=512)[:, c0:c0 + cnt, :],
                                in_=h_v[:, c0:c0 + cnt, t4 * 512:(t4 + 1) * 512],
                            ).then_inc(S("HRL"), 16)
                        sy.wait_ge(S("WOM"), 64 * 8 * t4 + 64)
                        for piece in range(4):
                            sy.dma_start(
                                out=at4[:].rearrange("p (c t) -> p c t", t=512)[:, piece * 8:(piece + 1) * 8, :],
                                in_=at_v4[:, piece * 8:(piece + 1) * 8, t4 * 512:(t4 + 1) * 512],
                            ).then_inc(S("AT4"), 16)
                        for hg in range(8):
                            G = t4 * 8 + hg
                            for tsub in range(4):
                                if 4 * G + tsub >= 2:
                                    sy.wait_ge(S("CMB"), 4 * G + tsub + 3)
                                sy.dma_start(
                                    out=xb[tsub % 2][:],
                                    in_=x_in[t4 * 512 + tsub * 128: t4 * 512 + (tsub + 1) * 128,
                                             hg * 512:(hg + 1) * 512],
                                ).then_inc(S2("XL3", tsub % 2), 16)

                @blk.gpsimd
                def _(g):
                    g.wait_ge(S2("ATS", 0), ATS_ALL)
                    g.wait_ge(S2("ATS", 1), ATS_ALL)
                    g.sem_inc(S("DMM"), 2)
                    g.sem_inc(S("SCL"), 4)
                    g.sem_inc(S("TH"), 4)
                    g.sem_inc(S("WOM"), 64)
                    g.sem_inc(S("CMB"), 4)

                    def em_dt(ghg, igs):
                        t4, hg = divmod(ghg, 8)
                        if t4 >= 4:
                            return
                        for ig in igs:
                            cnt = 8 if ig < 10 else 3
                            par = ig % 2
                            g.wait_ge(S("DMM"), ghg * NGI + ig + 1)
                            g.dma_start(
                                out=dtb2[par][:].rearrange("p (c h) -> p c h", h=512)[:, 0:cnt, :],
                                in_=dt_v[:, ig_base(ig):ig_base(ig) + cnt,
                                         hg * 512:(hg + 1) * 512],
                            ).then_inc(S2("DTB", par), 16)

                    def em_wo(ghg):
                        t4, hg = divmod(ghg, 8)
                        for ps in range(2):
                            for ck in range(CK):
                                g.wait_ge(S("WOM"), 64 * ghg + ps * 32 + ck + 63)
                                g.dma_start(
                                    out=wo_buf[ck % 2][:],
                                    in_=woT[ck * 128:(ck + 1) * 128, hg * 512:(hg + 1) * 512],
                                ).then_inc(S2("WOL", ck % 2), 16)

                    em_dt(0, range(0, 2))
                    for ghg in range(32):
                        em_dt(ghg, range(2, NGI))
                        em_dt(ghg + 1, range(0, 2))
                        em_wo(ghg)

                @blk.tensor
                def _(t):
                    R = SR["tensor"]
                    t.wait_ge(S2("ATS", 0), ATS_ALL)
                    t.wait_ge(S2("ATS", 1), ATS_ALL)
                    with fori(nc, SR["tensor"], 0, 4) as t4v:
                        with fori(nc, SR["tensor"], 0, 8) as hg:
                            R.wait(S("SIL"), [(t4v, 32), (hg, 4)], 0)
                            for ig in range(NGI):
                                cnt = 8 if ig < 10 else 3
                                par = ig % 2
                                R.wait(S2("DTB", par), [(t4v, 16 * 48), (hg, 16 * 6)], 16 * (ig // 2 + 1))
                                pneed = min((ig_base(ig) + cnt - 1) // 11, 7)
                                R.wait(S("HRL"), [(t4v, 128)], 16 * (pneed + 1))
                                for j in range(cnt):
                                    cchunk = ig_base(ig) + j
                                    for hc in range(4):
                                        mm = t.matmul(pd[:, hc * 512:(hc + 1) * 512],
                                                      dtb2[par][:, j * 512 + hc * 128: j * 512 + hc * 128 + 128],
                                                      h_res[:, cchunk * 512:(cchunk + 1) * 512],
                                                      start=(ig == 0 and j == 0),
                                                      stop=(ig == NGI - 1 and j == cnt - 1))
                                mm.then_inc(S("DMM"), 1)
                            R.wait(S("AT4"), [(t4v, 64)], 64)
                            for ps in range(2):
                                R.wait(S("TH"), [(t4v, 32), (hg, 4)], 2 * ps + 4)
                                for ck in range(CK):
                                    R.wait(S2("WOL", ck % 2), [(t4v, 16 * 256), (hg, 16 * 32)],
                                           16 * (ps * 16 + ck // 2 + 1))
                                    for hx in range(2):
                                        hc = ps * 2 + hx
                                        mm = t.matmul(p_wf[:, hx * 512:(hx + 1) * 512],
                                                      wo_buf[ck % 2][:, hc * 128:(hc + 1) * 128],
                                                      at4[:, ck * 512:(ck + 1) * 512],
                                                      start=(ck == 0), stop=(ck == CK - 1))
                                    mm.then_inc(S("WOM"), 1)
                            R.wait(S("TH"), [(t4v, 32), (hg, 4)], 8)
                            R.wait(S("CMB"), [(t4v, 32), (hg, 4)], 4)
                            for hc in range(4):
                                for tsub in range(4):
                                    tr = t.transpose(ptrT[:, tsub * 512 + hc * 128: tsub * 512 + (hc + 1) * 128],
                                                     prodb[:, hc * 512 + tsub * 128: hc * 512 + (tsub + 1) * 128],
                                                     sid[:])
                            tr.then_inc(S("TR4"), 1)

                @blk.vector
                def _(v):
                    for t4 in range(4):
                        for hg in range(8):
                            G = t4 * 8 + hg
                            for hc in range(4):
                                v.wait_ge(S("DMM"), (G + 1) * NGI + 2)
                                if hc == 0:
                                    v.wait_ge(S("SCL"), 4 * G + 4)
                                v.tensor_scalar_mul(stmp[:, hc * 512:(hc + 1) * 512],
                                                    pd[:, hc * 512:(hc + 1) * 512],
                                                    dsc[:, hg * 4 + hc:hg * 4 + hc + 1]).then_inc(S("SIL"), 1)
                            if G >= 1:
                                v.wait_ge(S("TR4"), G)
                            for ps in range(2):
                                v.wait_ge(S("WOM"), 64 * G + ps * 32 + 32 + 64)
                                for hx in range(2):
                                    hc = ps * 2 + hx
                                    v.wait_ge(S("SCL"), 4 * G + hc + 1 + 4)
                                    v.tensor_mul(prodb[:, hc * 512:(hc + 1) * 512],
                                                 p_wf[:, hx * 512:(hx + 1) * 512],
                                                 gst[:, hc * 512:(hc + 1) * 512]).then_inc(S("TH"), 1)
                            v.wait_ge(S("TR4"), G + 1)
                            for tsub in range(4):
                                idx = 4 * G + tsub
                                v.wait_ge(S2("XL3", tsub % 2), 16 * (idx // 2 + 1))
                                if idx >= 2:
                                    v.wait_ge(S2("OUT", tsub % 2), 16 * (idx // 2))
                                v.tensor_add(ch_f[tsub % 2][:],
                                             ptrT[:, tsub * 512:(tsub + 1) * 512],
                                             xb[tsub % 2][:]).then_inc(S("CMB"), 1)

                @blk.scalar
                def _(sc):
                    for t4 in range(4):
                        for hg in range(8):
                            G = t4 * 8 + hg
                            for hc in range(4):
                                sc.wait_ge(S("SIL"), 4 * G + hc + 1)
                                if hc == 0:
                                    sc.wait_ge(S("TH"), 4 * G + 4)
                                sc.activation(gst[:, hc * 512:(hc + 1) * 512],
                                              stmp[:, hc * 512:(hc + 1) * 512],
                                              AF.Silu).then_inc(S("SCL"), 1)
                            for tsub in range(4):
                                sc.wait_ge(S("CMB"), 4 * G + tsub + 5)
                                sc.dma_start(
                                    out=out[t4 * 512 + tsub * 128: t4 * 512 + (tsub + 1) * 128,
                                            hg * 512:(hg + 1) * 512],
                                    in_=ch_f[tsub % 2][:],
                                ).then_inc(S2("OUT", tsub % 2), 16)

        with nc.Block("tail") as blk:
            @blk.sync
            def _(sy):
                if lvl >= 3:
                    sy.wait_ge(S2("OUT", 0), 16 * 64)
                    sy.wait_ge(S2("OUT", 1), 16 * 64)
                elif lvl >= 2:
                    if p2_qts > 0:
                        sy.wait_ge(S2("ATS", 0), 16 * 4 * p2_qts + 16)
                        sy.wait_ge(S2("ATS", 1), 16 * 4 * p2_qts + 16)
                    else:
                        sy.wait_ge(S("VTE"), 32)
                elif lvl >= 1:
                    sy.wait_ge(S2("HST", 0), 16 * 43)
                    sy.wait_ge(S2("HST", 1), 16 * 43)
                    sy.wait_ge(S2("DTS", 0), 16 * 43)
                    sy.wait_ge(S2("DTS", 1), 16 * 43)
                else:
                    sy.wait_ge(S2("QST", 0), 16 * 24)
                    sy.wait_ge(S2("QST", 1), 16 * 24)

    return nc


def _prep(inputs):
    x = inputs["L0_attn_hidden_states"]
    B = x.shape[0]
    wqT = np.ascontiguousarray(np.asarray(inputs["L0_attn_Wq"]).T)
    wkT = np.ascontiguousarray(np.asarray(inputs["L0_attn_Wk"]).T)
    wvT = np.ascontiguousarray(np.asarray(inputs["L0_attn_Wv"]).T)
    woT = np.ascontiguousarray(np.asarray(inputs["L0_attn_Wo"]).T)
    gT = np.ascontiguousarray(
        np.asarray(inputs["L0_ffn_gate_wq"]).T.reshape(H, CI, 128).transpose(1, 0, 2))
    uT = np.ascontiguousarray(
        np.asarray(inputs["L0_ffn_up_wq"]).T.reshape(H, CI, 128).transpose(1, 0, 2))
    dT = np.ascontiguousarray(np.asarray(inputs["L0_ffn_down_wq"]).T).reshape(CI, 128, H)
    gsc = np.ascontiguousarray(np.asarray(inputs["L0_ffn_gate_scale"]).reshape(CI, 128).T)
    usc = np.ascontiguousarray(np.asarray(inputs["L0_ffn_up_scale"]).reshape(CI, 128).T)
    dsc = np.ascontiguousarray(np.asarray(inputs["L0_ffn_down_scale"]).reshape(CK, 128).T)

    inv = (1.0 / (10000.0 ** (np.arange(0, 128, 2, dtype=np.float32) / 128.0))).astype(np.float32)
    pos = np.arange(T, dtype=np.float32)
    freqs = pos[:, None] * inv[None, :]
    emb = np.concatenate([freqs, freqs], axis=1)
    cosT = np.ascontiguousarray(np.cos(emb).astype(np.float32).T)
    sin = np.sin(emb).astype(np.float32)
    s2 = np.ascontiguousarray(np.concatenate([-sin[:, :64], sin[:, 64:]], axis=1).T)
    isq = np.float32(1.0 / np.sqrt(128.0))
    cosQ = np.ascontiguousarray(cosT * isq)
    sinQ = np.ascontiguousarray(s2 * isq)
    ident = np.eye(128, dtype=ml_dtypes.bfloat16)
    q = np.arange(128)
    maskd = np.where(q[None, :] <= q[:, None], 0.0, NEG).astype(np.float32)
    maskb = maskd.astype(ml_dtypes.bfloat16)

    shared = dict(wqT=wqT, wkT=wkT, wvT=wvT, woT=woT, gT=gT, uT=uT, dT=dT,
                  gsc=gsc, usc=usc, dsc=dsc, cosT=cosT, sinN=s2,
                  cosQ=cosQ, sinQ=sinQ, ident=ident, maskd=maskd, maskb=maskb)
    in_maps = []
    for c in range(B):
        m = dict(shared)
        m["x"] = np.ascontiguousarray(np.asarray(x[c], dtype=np.float32))
        in_maps.append(m)
    return in_maps


def kernel(**inputs):
    in_maps = _prep(inputs)
    nc = build()
    res = run_bass_kernel_spmd(nc, in_maps, core_ids=list(range(len(in_maps))))
    return np.stack([np.asarray(r["out"], dtype=np.float32) for r in res.results], axis=0)


# version 07
# speedup vs baseline: 1.4305x; 1.4305x over previous
import numpy as np
import ml_dtypes
from contextlib import ExitStack

import concourse.bass as bass
import concourse.mybir as mybir
from contextlib import contextmanager
from concourse.bass_primitives import RegisterHandles
from concourse.expressions import RuntimeValue
from concourse.ordered_engine_set import OrderedEngineSet
from concourse.bass_utils import run_bass_kernel_spmd

F32 = mybir.dt.float32
BF16 = mybir.dt.bfloat16
I32 = mybir.dt.int32
AF = mybir.ActivationFunctionType
ALU = mybir.AluOpType
AX = mybir.AxisListType
ds = bass.ds

H = 4096
T = 2048
II = 11008
CK = 32
CI = 86
QT = 16
KVH = 1024
EPS = 1e-5
NEG = -1.0e30
NGI = 12


class Scratch:

    def __init__(self, eng, nrot=4, nslot=8):
        self.eng = eng
        ename = str(eng.engine.value)
        self.rot = [eng.alloc_register(f"sr_{ename}_{i}") for i in range(nrot)]
        self.slot = [eng.alloc_register(f"so_{ename}_{i}") for i in range(nslot)]
        self.tmp = eng.alloc_register(f"st_{ename}")
        self.loops = [eng.alloc_register(f"lp_{ename}_{i}") for i in range(3)]
        self.depth = 0
        self.i = 0
        self.vcache = {}

    def _lin(self, r, terms, const):
        first = True
        for var, coeff in terms:
            if first:
                if coeff == 1:
                    self.eng.reg_mov(r, var)
                else:
                    self.eng.reg_alu(r, var, coeff, ALU.mult)
                first = False
            else:
                if coeff == 1:
                    self.eng.reg_alu(r, r, var, ALU.add)
                else:
                    self.eng.reg_alu(self.tmp, var, coeff, ALU.mult)
                    self.eng.reg_alu(r, r, self.tmp, ALU.add)
        if const:
            self.eng.reg_alu(r, r, const, ALU.add)
        return r

    def wait(self, sem, terms, const=0):
        terms = [(v, c) for (v, c) in terms if c != 0 and v is not None]
        if not terms:
            if const > 0:
                self.eng.wait_ge(sem, const)
            return
        r = self.rot[self.i]
        self.i = (self.i + 1) % len(self.rot)
        self.eng.wait_ge(sem, self._lin(r, terms, const))

    def val(self, slot, terms, const, lo, hi):
        terms = [(v, c) for (v, c) in terms if c != 0 and v is not None]
        if not terms:
            return const
        r = self._lin(self.slot[slot], terms, const)
        return RuntimeValue(RegisterHandles([r]), min_val=lo, max_val=hi)


@contextmanager
def fori(nc, R, start, end):
    eng = R.eng
    r = R.loops[R.depth]
    R.depth += 1
    eng.reg_mov(r, start)
    lid = nc.next_id()
    ls, le = f"fori{lid}_loop", f"fori{lid}_end"
    eng.br(ls)
    engines = OrderedEngineSet([eng.engine])
    with nc.body(ls, valid_engines=engines):
        yield RuntimeValue(RegisterHandles([r]), min_val=start, max_val=end - 1)
        eng.reg_alu(r, r, 1, ALU.add)
        eng.br_cmp(r, end, ls, le, "IS_LT")
    nc.switch_bb(le)
    R.depth -= 1


def build(dbg=(), upto="full", p2_qts=QT, detect_races=True):
    order = ["p1", "p3a", "p2", "p2b", "full"]
    lvl = order.index(upto)
    nc = bass.Bass("TRN2", target_bir_lowering=False,
                   detect_race_conditions=detect_races)

    def scratch_kind(name):
        return "ExternalOutput" if name in dbg else "Internal"

    x_in = nc.dram_tensor("x", [T, H], F32, kind="ExternalInput").ap()
    wqT = nc.dram_tensor("wqT", [H, H], F32, kind="ExternalInput").ap()
    wkT = nc.dram_tensor("wkT", [H, KVH], F32, kind="ExternalInput").ap()
    wvT = nc.dram_tensor("wvT", [H, KVH], F32, kind="ExternalInput").ap()
    woT = nc.dram_tensor("woT", [H, H], F32, kind="ExternalInput").ap()
    gT = nc.dram_tensor("gT", [CI, H, 128], I32, kind="ExternalInput").ap()
    uT = nc.dram_tensor("uT", [CI, H, 128], I32, kind="ExternalInput").ap()
    dT = nc.dram_tensor("dT", [CI, 128, H], I32, kind="ExternalInput").ap()
    gsc_in = nc.dram_tensor("gsc", [128, CI], F32, kind="ExternalInput").ap()
    usc_in = nc.dram_tensor("usc", [128, CI], F32, kind="ExternalInput").ap()
    dsc_in = nc.dram_tensor("dsc", [128, CK], F32, kind="ExternalInput").ap()
    cos_in = nc.dram_tensor("cosT", [128, T], F32, kind="ExternalInput").ap()
    sin_in = nc.dram_tensor("sinN", [128, T], F32, kind="ExternalInput").ap()
    cosq_in = nc.dram_tensor("cosQ", [128, T], F32, kind="ExternalInput").ap()
    sinq_in = nc.dram_tensor("sinQ", [128, T], F32, kind="ExternalInput").ap()
    id_in = nc.dram_tensor("ident", [128, 128], BF16, kind="ExternalInput").ap()
    mk_in = nc.dram_tensor("maskd", [128, 128], F32, kind="ExternalInput").ap()
    mkb_in = nc.dram_tensor("maskb", [128, 128], BF16, kind="ExternalInput").ap()

    q_sc = nc.dram_tensor("q_sc", [CK, 128, T], BF16, kind=scratch_kind("q_sc")).ap()
    k_sc = nc.dram_tensor("k_sc", [8, 128, T], BF16, kind=scratch_kind("k_sc")).ap()
    v_sc = nc.dram_tensor("v_sc", [8, 128, T], BF16, kind=scratch_kind("v_sc")).ap()
    h_sc = nc.dram_tensor("h_sc", [CI, 128, T], BF16, kind=scratch_kind("h_sc")).ap()
    dt_sc = nc.dram_tensor("dt_sc", [CI, 128, H], BF16, kind=scratch_kind("dt_sc")).ap()
    at_sc = nc.dram_tensor("at_sc", [CK, 128, T], BF16, kind=scratch_kind("at_sc")).ap()
    s_sc = nc.dram_tensor("s_sc", [QT, 128, H], F32, kind=scratch_kind("s_sc")).ap()
    xn_dbg = None
    if "xn" in dbg:
        xn_dbg = nc.dram_tensor("xn_dbg", [CK, 128, T], BF16, kind="ExternalOutput").ap()
    out = nc.dram_tensor("out", [T, H], F32, kind="ExternalOutput").ap()

    SR = {
        "sync": Scratch(nc.sync, nslot=4), "tensor": Scratch(nc.tensor, nslot=2),
        "vector": Scratch(nc.vector, nslot=6), "scalar": Scratch(nc.scalar, nslot=2),
        "gpsimd": Scratch(nc.gpsimd, nslot=8),
    }

    with ExitStack() as top:
        E = top.enter_context
        gsc = E(nc.sbuf_tensor("gscs", [128, CI], F32))
        usc = E(nc.sbuf_tensor("uscs", [128, CI], F32))
        gsus = E(nc.sbuf_tensor("gsus", [128, CI], F32))
        dsc = E(nc.sbuf_tensor("dscs", [128, CK], F32))
        sid = E(nc.sbuf_tensor("sid", [128, 128], BF16))
        smask = E(nc.sbuf_tensor("smask", [128, 128], F32))
        smaskb = E(nc.sbuf_tensor("smaskb", [128, 128], BF16))
        epsc = E(nc.sbuf_tensor("epsc", [128, 1], F32))

        sems = {}
        for name in [
            "PRE", "SQ1", "SX1", "VARS", "RSTD", "NMR",
            "NRM", "TRP", "XNE", "VC", "SC",
            "PMM", "PFREE", "QBF",
            "PE3", "UEV", "HS",
            "KVL", "VTR", "VTE",
            "SMM", "MX", "EXP", "PB", "PT", "PTE", "PV", "ATE", "RC",
            "HRL", "DMM", "SIL", "SCL", "CMB",
            "WOM", "TH", "AT4", "TR4",
        ] + [f"WS{j}" for j in range(16)] + [
        ]:
            sems[name] = E(nc.semaphore(name))
        for name in ["X1", "X2", "WL", "QST", "GUL", "DTL", "DTS", "HST",
                     "QL", "DTB", "ATS", "XL3", "OUT"]:
            sems[name + "0"] = E(nc.semaphore(name + "0"))
            sems[name + "1"] = E(nc.semaphore(name + "1"))
        S = lambda n: sems[n]
        S2 = lambda n, p: sems[n + str(p)]

        with ExitStack() as ph:
            e = ph.enter_context
            xn = e(nc.sbuf_tensor("xn", [128, CK * T], BF16))

            with ExitStack() as pa:
                e2 = pa.enter_context
                xt = [e2(nc.sbuf_tensor(f"xt{j}", [128, H], F32)) for j in range(2)]
                junk = e2(nc.sbuf_tensor("junk", [128, 3 * QT], F32))
                sx = e2(nc.sbuf_tensor("sx", [128, QT], F32))
                ssq = e2(nc.sbuf_tensor("ssq", [128, QT], F32))
                stat = e2(nc.sbuf_tensor("stat", [128, 4 * QT], F32))
                xnt = [e2(nc.sbuf_tensor(f"xnt{j}", [128, H], BF16)) for j in range(2)]
                ptr = [e2(nc.psum_tensor(f"ptrA{j}", [128, 1024], BF16)) for j in range(2)]
                mean = stat[:, 0:QT]
                var = stat[:, QT:2 * QT]
                rstd = stat[:, 2 * QT:3 * QT]
                nmr = stat[:, 3 * QT:4 * QT]

                with nc.Block("p1a") as blk:
                    @blk.sync
                    def _(sy):
                        sy.dma_start(out=sid[:], in_=id_in).then_inc(S("PRE"), 16)
                        sy.dma_start(out=smask[:], in_=mk_in).then_inc(S("PRE"), 16)
                        sy.dma_start(out=smaskb[:], in_=mkb_in).then_inc(S("PRE"), 16)
                        sy.dma_start(out=gsc[:], in_=gsc_in).then_inc(S("PRE"), 16)
                        sy.dma_start(out=usc[:], in_=usc_in).then_inc(S("PRE"), 16)
                        sy.dma_start(out=dsc[:], in_=dsc_in).then_inc(S("PRE"), 16)
                        for ti in range(QT):
                            if ti >= 2:
                                sy.wait_ge(S("SQ1"), ti - 1)
                                sy.wait_ge(S("SX1"), ti - 1)
                            sy.dma_start(out=xt[ti % 2][:],
                                         in_=x_in[ti * 128:(ti + 1) * 128, :]).then_inc(S2("X1", ti % 2), 16)
                        for ti in range(QT):
                            sy.wait_ge(S("SQ1"), QT - 1 + ti if ti < 2 else QT)
                            sy.wait_ge(S("SX1"), QT - 1 + ti if ti < 2 else QT)
                            if ti >= 2:
                                sy.wait_ge(S("NRM"), ti - 1)
                            sy.dma_start(out=xt[ti % 2][:],
                                         in_=x_in[ti * 128:(ti + 1) * 128, :]).then_inc(S2("X2", ti % 2), 16)

                    @blk.scalar
                    def _(sc):
                        for ti in range(QT):
                            sc.wait_ge(S2("X1", ti % 2), 16 * (ti // 2 + 1))
                            sc.activation(xnt[ti % 2][:], xt[ti % 2][:], AF.Square,
                                          accum_out=ssq[:, ti:ti + 1]).then_inc(S("SQ1"), 1)
                        sc.wait_ge(S("VARS"), 1)
                        sc.activation(junk[:, 0:QT], var, AF.Ln, bias=epsc[:, 0:1]).then_inc(S("SC"), 1)
                        sc.wait_ge(S("SC"), 1)
                        sc.activation(rstd, junk[:, 0:QT], AF.Exp, scale=-0.5).then_inc(S("RSTD"), 1)
                        for ti in range(QT):
                            sc.wait_ge(S2("X2", ti % 2), 16 * (ti // 2 + 1))
                            if ti == 0:
                                sc.wait_ge(S("NMR"), 1)
                            if ti >= 2:
                                sc.wait_ge(S("TRP"), 4 * (ti - 1))
                            sc.activation(xnt[ti % 2][:], xt[ti % 2][:], AF.Identity,
                                          scale=rstd[:, ti:ti + 1],
                                          bias=nmr[:, ti:ti + 1]).then_inc(S("NRM"), 1)

                    @blk.vector
                    def _(v):
                        v.memset(epsc[:], EPS)
                        for ti in range(QT):
                            v.wait_ge(S2("X1", ti % 2), 16 * (ti // 2 + 1))
                            v.reduce_sum(sx[:, ti:ti + 1], xt[ti % 2][:], axis=AX.X).then_inc(S("SX1"), 1)
                        v.wait_ge(S("SQ1"), QT)
                        v.wait_ge(S("SX1"), QT)
                        v.tensor_scalar_mul(mean, sx[:], 1.0 / H)
                        v.tensor_scalar_mul(var, ssq[:], 1.0 / H).then_inc(S("VC"), 1)
                        v.wait_ge(S("VC"), 1)
                        v.tensor_tensor(junk[:, QT:2 * QT], mean, mean, ALU.mult)
                        v.tensor_scalar_mul(junk[:, 2 * QT:3 * QT], mean, -1.0).then_inc(S("VC"), 1)
                        v.wait_ge(S("VC"), 2)
                        v.tensor_tensor(var, var, junk[:, QT:2 * QT], ALU.subtract).then_inc(S("VARS"), 1)
                        v.wait_ge(S("RSTD"), 1)
                        v.tensor_tensor(nmr, junk[:, 2 * QT:3 * QT], rstd, ALU.mult).then_inc(S("NMR"), 1)
                        for ti in range(QT):
                            for grp in range(4):
                                v.wait_ge(S("TRP"), 4 * ti + grp + 1)
                                v.tensor_copy(
                                    out=xn[:].rearrange("p (c t) -> p c t", t=T)[:, grp * 8:(grp + 1) * 8, ti * 128:(ti + 1) * 128],
                                    in_=ptr[grp % 2][:].rearrange("p (c t) -> p c t", t=128),
                                ).then_inc(S("XNE"), 1)

                    @blk.tensor
                    def _(t):
                        t.wait_ge(S("PRE"), 16 * 6)
                        for ti in range(QT):
                            t.wait_ge(S("NRM"), ti + 1)
                            for grp in range(4):
                                gidx = 4 * ti + grp
                                if gidx >= 2:
                                    t.wait_ge(S("XNE"), gidx - 1)
                                for k in range(8):
                                    c = grp * 8 + k
                                    tr = t.transpose(ptr[grp % 2][:, k * 128:(k + 1) * 128],
                                                     xnt[ti % 2][:, c * 128:(c + 1) * 128], sid[:])
                                tr.then_inc(S("TRP"), 1)

            with ExitStack() as pb:
                e2 = pb.enter_context
                cosT = e2(nc.sbuf_tensor("cosTs", [128, T], F32))
                sinN = e2(nc.sbuf_tensor("sinNs", [128, T], F32))
                cosQ = e2(nc.sbuf_tensor("cosQs", [128, T], F32))
                sinQ = e2(nc.sbuf_tensor("sinQs", [128, T], F32))
                wbuf = [e2(nc.sbuf_tensor(f"wb{j}", [128, H], BF16)) for j in range(2)]
                qtmp = e2(nc.sbuf_tensor("qtmp", [128, T], F32))
                t2 = e2(nc.sbuf_tensor("t2", [128, T], F32))
                qbf = [e2(nc.sbuf_tensor(f"qbf{j}", [128, T], BF16)) for j in range(2)]
                pq = [e2(nc.psum_tensor(f"pq{j}", [128, T], F32)) for j in range(2)]

                sections = [(wqT, CK, q_sc, True), (wkT, 8, k_sc, True), (wvT, 8, v_sc, False)]
                bases = [0, CK, CK + 8]

                def em_wload(g, R, base, cv, par, wsrc, first=False):
                    var, coeff, cst = cv
                    if not first:
                        R.wait(S("PMM"), [(var, coeff)], cst + base - 1)
                    nch = wsrc.shape[1] // 128
                    off = R.val(par, [(var, coeff * 128)], cst * 128, 0, (nch - 1) * 128)
                    g.dma_start(
                        out=wbuf[par][:].rearrange("p (c d) -> p c d", d=128),
                        in_=wsrc.rearrange("(c p) d -> p c d", p=128)[:, :, ds(off, 128)],
                    ).then_inc(S2("WL", par), 16)

                def em_wmm(t, R, base, cv, par, first=False):
                    var, coeff, cst = cv
                    R.wait(S2("WL", par), [(var, 16)], 16 * (base // 2) + 16)
                    if not first:
                        R.wait(S("PFREE"), [(var, coeff)], cst + base - 1)
                    for ck in range(CK):
                        for ns in range(4):
                            mm = t.matmul(
                                pq[par][:, ns * 512:(ns + 1) * 512],
                                wbuf[par][:, ck * 128:(ck + 1) * 128],
                                xn[:, ck * T + ns * 512: ck * T + (ns + 1) * 512],
                                start=(ck == 0), stop=(ck == CK - 1),
                            )
                    mm.then_inc(S("PMM"), 1)

                def em_rope(v, R, base, cv, par, rope, first=False, tabs=None):
                    ct, st = tabs if tabs is not None else (cosT, sinN)
                    var, coeff, cst = cv
                    R.wait(S("PMM"), [(var, coeff)], cst + base + 1)
                    if not first:
                        R.wait(S2("QST", par), [(var, 16)], 16 * (base // 2))
                    if rope:
                        if not (isinstance(cst, int) and var is None and cst + base == 0):
                            R.wait(S("QBF"), [(var, coeff)], cst + base)
                        v.tensor_tensor(qtmp[:], pq[par][:], ct[:], ALU.mult)
                        v.tensor_tensor(t2[0:64, :], pq[par][64:128, :], st[0:64, :], ALU.mult)
                        v.tensor_tensor(t2[64:128, :], pq[par][0:64, :], st[64:128, :],
                                        ALU.mult).then_inc(S("PFREE"), 1)
                        R.wait(S("PFREE"), [(var, coeff)], cst + base + 1)
                        v.tensor_tensor(qbf[par][:], qtmp[:], t2[:], ALU.add).then_inc(S("QBF"), 1)
                    else:
                        v.tensor_copy(out=qbf[par][:], in_=pq[par][:]).then_inc(S("PFREE"), 1)

                def em_store(sc, R, base, cv, par, dst, rope, n):
                    var, coeff, cst = cv
                    R.wait(S("QBF") if rope else S("PFREE"), [(var, coeff)], cst + base + 1)
                    off = R.val(par, [(var, coeff)], cst, 0, n - 1)
                    sc.dma_start(out=dst[ds(off, 1)].rearrange("n p t -> p (n t)"),
                                 in_=qbf[par][:]).then_inc(S2("QST", par), 16)

                with nc.Block("p1b") as blk:
                    @blk.sync
                    def _(sy):
                        sy.wait_ge(S("XNE"), 4 * QT)
                        sy.dma_start(out=cosT[:], in_=cos_in).then_inc(S("PRE"), 16)
                        sy.dma_start(out=sinN[:], in_=sin_in).then_inc(S("PRE"), 16)
                        sy.dma_start(out=cosQ[:], in_=cosq_in).then_inc(S("PRE"), 16)
                        sy.dma_start(out=sinQ[:], in_=sinq_in).then_inc(S("PRE"), 16)

                    @blk.gpsimd
                    def _(g):
                        R = SR["gpsimd"]
                        g.wait_ge(S("XNE"), 4 * QT)
                        em_wload(g, R, 0, (None, 0, 0), 0, wqT, first=True)
                        em_wload(g, R, 0, (None, 0, 1), 1, wqT, first=True)
                        with fori(nc, SR["gpsimd"], 1, CK // 2) as cp:
                            em_wload(g, R, 0, (cp, 2, 0), 0, wqT)
                            em_wload(g, R, 0, (cp, 2, 1), 1, wqT)
                        for si in (1, 2):
                            wsrc, n, _, _ = sections[si]
                            with fori(nc, SR["gpsimd"], 0, n // 2) as cp:
                                em_wload(g, R, bases[si], (cp, 2, 0), 0, wsrc)
                                em_wload(g, R, bases[si], (cp, 2, 1), 1, wsrc)

                    @blk.tensor
                    def _(t):
                        R = SR["tensor"]
                        t.wait_ge(S("XNE"), 4 * QT)
                        em_wmm(t, R, 0, (None, 0, 0), 0, first=True)
                        em_wmm(t, R, 0, (None, 0, 1), 1, first=True)
                        with fori(nc, SR["tensor"], 1, CK // 2) as cp:
                            em_wmm(t, R, 0, (cp, 2, 0), 0)
                            em_wmm(t, R, 0, (cp, 2, 1), 1)
                        for si in (1, 2):
                            n = sections[si][1]
                            with fori(nc, SR["tensor"], 0, n // 2) as cp:
                                em_wmm(t, R, bases[si], (cp, 2, 0), 0)
                                em_wmm(t, R, bases[si], (cp, 2, 1), 1)

                    @blk.vector
                    def _(v):
                        R = SR["vector"]
                        v.wait_ge(S("PRE"), 16 * 10)
                        qt_tabs = (cosQ, sinQ)
                        em_rope(v, R, 0, (None, 0, 0), 0, True, first=True, tabs=qt_tabs)
                        em_rope(v, R, 0, (None, 0, 1), 1, True, first=True, tabs=qt_tabs)
                        with fori(nc, SR["vector"], 1, CK // 2) as cp:
                            em_rope(v, R, 0, (cp, 2, 0), 0, True, tabs=qt_tabs)
                            em_rope(v, R, 0, (cp, 2, 1), 1, True, tabs=qt_tabs)
                        for si in (1, 2):
                            n = sections[si][1]
                            rope = sections[si][3]
                            with fori(nc, SR["vector"], 0, n // 2) as cp:
                                em_rope(v, R, bases[si], (cp, 2, 0), 0, rope)
                                em_rope(v, R, bases[si], (cp, 2, 1), 1, rope)

                    @blk.scalar
                    def _(sc):
                        R = SR["scalar"]
                        em_store(sc, R, 0, (None, 0, 0), 0, q_sc, True, CK)
                        em_store(sc, R, 0, (None, 0, 1), 1, q_sc, True, CK)
                        with fori(nc, SR["scalar"], 1, CK // 2) as cp:
                            em_store(sc, R, 0, (cp, 2, 0), 0, q_sc, True, CK)
                            em_store(sc, R, 0, (cp, 2, 1), 1, q_sc, True, CK)
                        for si in (1, 2):
                            n = sections[si][1]
                            dst = sections[si][2]
                            rope = sections[si][3]
                            with fori(nc, SR["scalar"], 0, n // 2) as cp:
                                em_store(sc, R, bases[si], (cp, 2, 0), 0, dst, rope, n)
                                em_store(sc, R, bases[si], (cp, 2, 1), 1, dst, rope, n)

            with ExitStack() as pc:
              if lvl >= 1:
                e2 = pc.enter_context
                gbuf = [e2(nc.sbuf_tensor(f"gb{j}", [128, H], BF16)) for j in range(2)]
                ubuf = [e2(nc.sbuf_tensor(f"ub{j}", [128, H], BF16)) for j in range(2)]
                uev = e2(nc.sbuf_tensor("uev", [128, T], F32))
                hst = [e2(nc.sbuf_tensor(f"hst{j}", [128, T], BF16)) for j in range(2)]
                dtb = [e2(nc.sbuf_tensor(f"dtb{j}", [128, H], BF16)) for j in range(2)]
                pg = e2(nc.psum_tensor("pg", [128, T], F32))
                pu = e2(nc.psum_tensor("pu", [128, T], F32))

                def em3_load(g, R, cv, par, first=False):
                    var, coeff, cst = cv
                    if not first:
                        R.wait(S("PE3"), [(var, 2 * coeff)], 2 * cst - 2)
                    sb = par * 4
                    off = R.val(sb + 0, [(var, coeff)], cst, 0, CI - 1)
                    g.dma_start(out=gbuf[par][:].rearrange("p (c i) -> p c i", i=128),
                                in_=gT[ds(off, 1)].rearrange("n (c p) i -> p (n c) i", p=128)).then_inc(S2("GUL", par), 16)
                    off = R.val(sb + 1, [(var, coeff)], cst, 0, CI - 1)
                    g.dma_start(out=ubuf[par][:].rearrange("p (c i) -> p c i", i=128),
                                in_=uT[ds(off, 1)].rearrange("n (c p) i -> p (n c) i", p=128)).then_inc(S2("GUL", par), 16)
                    if not first:
                        R.wait(S2("DTS", par), [(var, 16)], 0)
                    off = R.val(sb + 2, [(var, coeff)], cst, 0, CI - 1)
                    g.dma_start(out=dtb[par][:], in_=dT[ds(off, 1)].rearrange("n p h -> p (n h)")).then_inc(S2("DTL", par), 16)

                def em3_mm(t, R, cv, par, first=False):
                    var, coeff, cst = cv
                    R.wait(S2("GUL", par), [(var, 32)], 32)
                    for th in range(2):
                        if not first:
                            R.wait(S("HS"), [(var, 2 * coeff)], 2 * cst + th - 1)
                        for w in range(2):
                            buf = gbuf[par] if w == 0 else ubuf[par]
                            ps = pg if w == 0 else pu
                            for ck in range(CK):
                                for ns in range(2):
                                    mm = t.matmul(
                                        ps[:, th * 1024 + ns * 512: th * 1024 + (ns + 1) * 512],
                                        buf[:, ck * 128:(ck + 1) * 128],
                                        xn[:, ck * T + th * 1024 + ns * 512: ck * T + th * 1024 + (ns + 1) * 512],
                                        start=(ck == 0), stop=(ck == CK - 1),
                                    )
                            if w == 1:
                                mm.then_inc(S("PE3"), 1)

                def em3_uev(sc, R, cv, par):
                    var, coeff, cst = cv
                    for th in range(2):
                        R.wait(S("PE3"), [(var, 2 * coeff)], 2 * cst + th + 1)
                        sc.activation(uev[:, th * 1024:(th + 1) * 1024],
                                      pu[:, th * 1024:(th + 1) * 1024], AF.Copy).then_inc(S("UEV"), 1)

                def em3_h(v, R, cv, par, first=False):
                    var, coeff, cst = cv
                    for th in range(2):
                        R.wait(S("UEV"), [(var, 2 * coeff)], 2 * cst + th + 1)
                        if not first:
                            R.wait(S2("HST", par), [(var, 16)], 0)
                        off = R.val(par, [(var, coeff)], cst, 0, CI - 1)
                        v.scalar_tensor_tensor(
                            hst[par][:, th * 1024:(th + 1) * 1024],
                            pg[:, th * 1024:(th + 1) * 1024],
                            gsus[:, ds(off, 1)],
                            uev[:, th * 1024:(th + 1) * 1024],
                            ALU.mult, ALU.mult,
                        ).then_inc(S("HS"), 1)

                def em3_store(sy, R, cv, par):
                    var, coeff, cst = cv
                    R.wait(S("HS"), [(var, 2 * coeff)], 2 * cst + 2)
                    off = R.val(par * 2, [(var, coeff)], cst, 0, CI - 1)
                    sy.dma_start(out=h_sc[ds(off, 1)].rearrange("n p t -> p (n t)"),
                                 in_=hst[par][:]).then_inc(S2("HST", par), 16)
                    R.wait(S2("DTL", par), [(var, 16)], 16)
                    off = R.val(par * 2 + 1, [(var, coeff)], cst, 0, CI - 1)
                    sy.dma_start(out=dt_sc[ds(off, 1)].rearrange("n p h -> p (n h)"),
                                 in_=dtb[par][:]).then_inc(S2("DTS", par), 16)

                with nc.Block("p3a") as blk:
                    @blk.gpsimd
                    def _(g):
                        R = SR["gpsimd"]
                        g.wait_ge(S2("QST", 0), 16 * 24)
                        g.wait_ge(S2("QST", 1), 16 * 24)
                        em3_load(g, R, (None, 0, 0), 0, first=True)
                        em3_load(g, R, (None, 0, 1), 1, first=True)
                        with fori(nc, SR["gpsimd"], 1, CI // 2) as cp:
                            em3_load(g, R, (cp, 2, 0), 0)
                            em3_load(g, R, (cp, 2, 1), 1)

                    @blk.tensor
                    def _(t):
                        R = SR["tensor"]
                        em3_mm(t, R, (None, 0, 0), 0, first=True)
                        em3_mm(t, R, (None, 0, 1), 1)
                        with fori(nc, SR["tensor"], 1, CI // 2) as cp:
                            em3_mm(t, R, (cp, 2, 0), 0)
                            em3_mm(t, R, (cp, 2, 1), 1)

                    @blk.scalar
                    def _(sc):
                        R = SR["scalar"]
                        em3_uev(sc, R, (None, 0, 0), 0)
                        em3_uev(sc, R, (None, 0, 1), 1)
                        with fori(nc, SR["scalar"], 1, CI // 2) as cp:
                            em3_uev(sc, R, (cp, 2, 0), 0)
                            em3_uev(sc, R, (cp, 2, 1), 1)

                    @blk.vector
                    def _(v):
                        R = SR["vector"]
                        v.wait_ge(S("PRE"), 16 * 6)
                        v.tensor_tensor(gsus[:], gsc[:], usc[:], ALU.mult).then_inc(S("VC"), 1)
                        v.wait_ge(S("VC"), 3)
                        em3_h(v, R, (None, 0, 0), 0, first=True)
                        em3_h(v, R, (None, 0, 1), 1, first=True)
                        with fori(nc, SR["vector"], 1, CI // 2) as cp:
                            em3_h(v, R, (cp, 2, 0), 0)
                            em3_h(v, R, (cp, 2, 1), 1)

                    @blk.sync
                    def _(sy):
                        R = SR["sync"]
                        em3_store(sy, R, (None, 0, 0), 0)
                        em3_store(sy, R, (None, 0, 1), 1)
                        with fori(nc, SR["sync"], 1, CI // 2) as cp:
                            em3_store(sy, R, (cp, 2, 0), 0)
                            em3_store(sy, R, (cp, 2, 1), 1)

            if xn_dbg is not None:
                with nc.Block("dbgxn") as blk0:
                    @blk0.scalar
                    def _(sc):
                        sc.wait_ge(S("XNE"), 4 * QT)
                        for c in range(CK):
                            sc.dma_start(out=xn_dbg[c], in_=xn[:, c * T:(c + 1) * T]).then_inc(S("CMB"), 16)
                        sc.wait_ge(S("CMB"), 16 * CK)

        with ExitStack() as ph:
          if lvl >= 2:
            e = ph.enter_context
            k_res = e(nc.sbuf_tensor("k_res", [128, 8 * T], BF16))
            v_tm = e(nc.sbuf_tensor("v_tm", [128, 16 * 1024], BF16))
            v_fm = e(nc.sbuf_tensor("v_fm", [128, 8 * T], BF16))
            q_res = [e(nc.sbuf_tensor(f"q_res{j}", [128, CK * 128], BF16)) for j in range(2)]
            pe_sb = [e(nc.sbuf_tensor(f"pe_sb{j}", [128, T], F32)) for j in range(2)]
            pb = [e(nc.sbuf_tensor(f"pbh{j}", [128, T], BF16)) for j in range(4)]
            pt_sb = [e(nc.sbuf_tensor(f"pt_sb{j}", [128, 512], BF16)) for j in range(2)]
            at_st = [e(nc.sbuf_tensor(f"at_st{j}", [128, 512], BF16)) for j in range(2)]
            nmb = [e(nc.sbuf_tensor(f"nmb{j}", [128, 1], F32)) for j in range(2)]
            lcol = [e(nc.sbuf_tensor(f"lcol{j}", [128, 1], F32)) for j in range(2)]
            rcol = [e(nc.sbuf_tensor(f"rcol{j}", [128, 1], F32)) for j in range(2)]
            p_s = e(nc.psum_tensor("p_s", [128, T], F32))
            p_o = [e(nc.psum_tensor(f"p_o{j}", [128, 512], F32)) for j in range(2)]
            ptr2 = [e(nc.psum_tensor(f"ptr2{j}", [128, 512], BF16)) for j in range(2)]

            at_v = at_sc.rearrange("c p t -> p c t")

            with nc.Block("p2s") as blk:
                @blk.sync
                def _(sy):
                    sy.wait_ge(S("PE3"), 2 * CI)
                    sy.wait_ge(S("HS"), 2 * CI)
                    sy.wait_ge(S2("HST", 0), 16 * 43)
                    sy.wait_ge(S2("HST", 1), 16 * 43)
                    sy.wait_ge(S2("DTS", 0), 16 * 43)
                    sy.wait_ge(S2("DTS", 1), 16 * 43)
                    sy.wait_ge(S2("QST", 0), 16 * 24)
                    sy.wait_ge(S2("QST", 1), 16 * 24)
                    sy.dma_start(out=k_res[:].rearrange("p (c t) -> p c t", t=T),
                                 in_=k_sc.rearrange("c p t -> p c t")).then_inc(S("KVL"), 16)
                    sy.dma_start(out=v_fm[:].rearrange("p (c t) -> p c t", t=T),
                                 in_=v_sc.rearrange("c p t -> p c t")).then_inc(S("KVL"), 16)

                @blk.vector
                def _(v):
                    v.sem_inc(S("ATE"), 2)
                    v.sem_inc(S2("ATS", 0), 16)
                    v.sem_inc(S2("ATS", 1), 16)
                    v.sem_inc(S("EXP"), 2)
                    v.sem_inc(S("PB"), 2)
                    v.sem_inc(S("PTE"), 2)
                    for grp in range(32):
                        v.wait_ge(S("VTR"), grp + 1)
                        kb, cq = grp // 2, grp % 2
                        v.tensor_copy(out=v_tm[:, kb * 1024 + cq * 512: kb * 1024 + (cq + 1) * 512],
                                      in_=ptr2[grp % 2][:]).then_inc(S("VTE"), 1)

                @blk.tensor
                def _(t):
                    t.wait_ge(S("HS"), 2 * CI)
                    t.wait_ge(S("KVL"), 32)
                    for grp in range(32):
                        kb, cq = grp // 2, grp % 2
                        if grp >= 2:
                            t.wait_ge(S("VTE"), grp - 1)
                        for j in range(4):
                            c = cq * 4 + j
                            tr = t.transpose(ptr2[grp % 2][:, j * 128:(j + 1) * 128],
                                             v_fm[:, c * T + kb * 128: c * T + kb * 128 + 128], sid[:])
                        tr.then_inc(S("VTR"), 1)

            for qt in range(min(QT, p2_qts)):
                klen = (qt + 1) * 128
                nfull, rem = divmod(klen, 512)
                heads_base = 32 * qt
                pt_base = 8 * (qt * (qt + 1) // 2)
                hg_base = 8 * qt

                with nc.Block(f"p2q{qt}") as blk:
                    @blk.sync
                    def _(sy):
                        if qt >= 2:
                            sy.wait_ge(S("SMM"), 32 * (qt - 1))
                        sy.dma_start(out=q_res[qt % 2][:].rearrange("p (c t) -> p c t", t=128),
                                     in_=q_sc.rearrange("c p t -> p c t")[:, :, qt * 128:(qt + 1) * 128]
                                     ).then_inc(S2("QL", qt % 2), 16)

                    @blk.tensor
                    def _(t):
                        for hg in range(8):
                            hgp = hg % 2
                            for j in range(4):
                                g = heads_base + hg * 4 + j
                                sb_off = (g % 2) * 1024 if qt <= 7 else 0
                                t.wait_ge(S("EXP"), (g + 1) if qt <= 7 else (g + 2))
                                if j == 0:
                                    t.wait_ge(S2("QL", qt % 2), 16 * (qt // 2 + 1))
                                    t.wait_ge(S("PB"), g + 2)
                                lhs = q_res[qt % 2][:, (hg * 4 + j) * 128:(hg * 4 + j + 1) * 128]
                                for ns in range(nfull):
                                    mm = t.matmul(p_s[:, sb_off + ns * 512: sb_off + (ns + 1) * 512], lhs,
                                                  k_res[:, hg * T + ns * 512: hg * T + (ns + 1) * 512],
                                                  start=True, stop=(qt * 128 < ns * 512 or qt * 128 >= (ns + 1) * 512))
                                if rem:
                                    mm = t.matmul(p_s[:, sb_off + nfull * 512: sb_off + nfull * 512 + rem], lhs,
                                                  k_res[:, hg * T + nfull * 512: hg * T + nfull * 512 + rem],
                                                  start=True, stop=False)
                                mm = t.matmul(p_s[:, sb_off + qt * 128: sb_off + (qt + 1) * 128], sid[:], smaskb[:],
                                              start=False, stop=True)
                                mm.then_inc(S("SMM"), 1)
                            t.wait_ge(S("PB"), heads_base + hg * 4 + 4 + 2)
                            for kb in range(qt + 1):
                                G = pt_base + hg * (qt + 1) + kb
                                if kb < 2:
                                    t.wait_ge(S("PTE"), pt_base + hg * (qt + 1) + 2)
                                else:
                                    t.wait_ge(S("PTE"), G + 1)
                                for j in range(4):
                                    tr = t.transpose(ptr2[kb % 2][:, j * 128:(j + 1) * 128],
                                                     pb[j][:, kb * 128:(kb + 1) * 128], sid[:])
                                tr.then_inc(S("PT"), 1)
                                if kb >= 1:
                                    t.wait_ge(S("PTE"), G + 2)
                                    if kb == 1:
                                        t.wait_ge(S("ATE"), hg_base + hg + 1)
                                    mm = t.matmul(p_o[hgp][:],
                                                  v_tm[:, (kb - 1) * 1024 + hg * 128: (kb - 1) * 1024 + hg * 128 + 128],
                                                  pt_sb[(kb - 1) % 2][:],
                                                  start=(kb == 1), stop=False)
                            t.wait_ge(S("PTE"), pt_base + (hg + 1) * (qt + 1) + 2)
                            if qt == 0:
                                t.wait_ge(S("ATE"), hg_base + hg + 1)
                            mm = t.matmul(p_o[hgp][:],
                                          v_tm[:, qt * 1024 + hg * 128: qt * 1024 + hg * 128 + 128],
                                          pt_sb[qt % 2][:],
                                          start=(qt == 0), stop=True)
                            mm.then_inc(S("PV"), 1)

                    @blk.vector
                    def _(v):
                        R = SR["vector"]
                        with fori(nc, SR["vector"], 0, 4) as hp:
                            for hgp in range(2):
                                def em_norm(j):
                                    gc = heads_base + hgp * 4 + j
                                    R.wait(S("EXP"), [(hp, 8)], gc + 3)
                                    v.reciprocal(rcol[j % 2][:], lcol[j % 2][:]).then_inc(S("RC"), 1)
                                    R.wait(S("RC"), [(hp, 8)], gc + 1)
                                    v.tensor_scalar_mul(pb[j][:, 0:klen], pe_sb[j % 2][:, 0:klen],
                                                        rcol[j % 2][:]).then_inc(S("PB"), 1)
                                for j in range(4):
                                    gc = heads_base + hgp * 4 + j
                                    sb_off = ((hgp * 4 + j) % 2) * 1024 if qt <= 7 else 0
                                    R.wait(S("SMM"), [(hp, 8)], gc + 1)
                                    R.wait(S("EXP"), [(hp, 8)], gc + 1)
                                    v.reduce_max(nmb[j % 2][:], p_s[:, sb_off:sb_off + klen], axis=AX.X,
                                                 negate=True).then_inc(S("MX"), 2)
                                    if j == 1:
                                        R.wait(S("PT"), [(hp, 2 * (qt + 1))], pt_base + hgp * (qt + 1))
                                    if j >= 1:
                                        em_norm(j - 1)
                                em_norm(3)
                                for kb in range(qt + 1):
                                    R.wait(S("PT"), [(hp, 2 * (qt + 1))], pt_base + hgp * (qt + 1) + kb + 1)
                                    v.tensor_copy(out=pt_sb[kb % 2][:], in_=ptr2[kb % 2][:]).then_inc(S("PTE"), 1)
                                R.wait(S("PV"), [(hp, 2)], hg_base + hgp + 1)
                                R.wait(S2("ATS", hgp), [(hp, 16)], 64 * qt + 16)
                                v.tensor_copy(out=at_st[hgp][:], in_=p_o[hgp][:]).then_inc(S("ATE"), 1)

                    @blk.scalar
                    def _(sc):
                        for hg in range(8):
                            hgp = hg % 2
                            for j in range(4):
                                g = heads_base + hg * 4 + j
                                sb_off = (g % 2) * 1024 if qt <= 7 else 0
                                sc.wait_ge(S("MX"), 2 * g + 2)
                                sc.wait_ge(S("PB"), g + 1)
                                sc.activation(pe_sb[j % 2][:, 0:klen], p_s[:, sb_off:sb_off + klen], AF.Exp,
                                              bias=nmb[j % 2][:],
                                              accum_out=lcol[j % 2][:]).then_inc(S("EXP"), 1)
                            sc.wait_ge(S("ATE"), hg_base + hg + 1 + 2)
                            sc.dma_start(
                                out=at_v[:, hg * 4:(hg + 1) * 4, qt * 128:(qt + 1) * 128],
                                in_=at_st[hgp][:].rearrange("p (j q) -> p j q", q=128),
                            ).then_inc(S2("ATS", hgp), 16)

        with ExitStack() as ph:
          if lvl >= 3:
            e = ph.enter_context
            h_res = e(nc.sbuf_tensor("h_res", [128, CI * 512], BF16))
            dtb2 = [e(nc.sbuf_tensor(f"dtb2{j}", [128, 8 * 512], BF16)) for j in range(2)]
            at4 = e(nc.sbuf_tensor("at4", [128, CK * 512], BF16))
            wo_buf = [e(nc.sbuf_tensor(f"wo{j}", [128, 256], BF16)) for j in range(16)]
            stmp = e(nc.sbuf_tensor("stmp", [128, 2048], F32))
            gst = e(nc.sbuf_tensor("gst", [128, 2048], BF16))
            prodb = e(nc.sbuf_tensor("prodb", [128, 2048], BF16))
            xb = [e(nc.sbuf_tensor(f"xb{j}", [128, 512], F32)) for j in range(2)]
            ch_f = [e(nc.sbuf_tensor(f"ch{j}", [128, 512], F32)) for j in range(2)]
            pd = e(nc.psum_tensor("pd", [128, 2048], F32))
            p_wf = e(nc.psum_tensor("p_wf", [128, 1024], F32))
            ptrT = e(nc.psum_tensor("ptrT", [128, 2048], BF16))

            h_v = h_sc.rearrange("c p t -> p c t")
            dt_v = dt_sc.rearrange("c p h -> p c h")
            at_v4 = at_sc.rearrange("c p t -> p c t")
            ATS_ALL = 16 * 64 + 16

            def ig_base(ig):
                return ig * 8 if ig < 10 else 80 + (ig - 10) * 3

            with nc.Block("p4") as blk:
                @blk.sync
                def _(sy):
                    sy.wait_ge(S2("ATS", 0), ATS_ALL)
                    sy.wait_ge(S2("ATS", 1), ATS_ALL)
                    for t4 in range(4):
                        for hg in range(8):
                            G = t4 * 8 + hg
                            for tsub in range(4):
                                if 4 * G + tsub >= 2:
                                    sy.wait_ge(S("CMB"), 4 * G + tsub + 3)
                                sy.dma_start(
                                    out=xb[tsub % 2][:],
                                    in_=x_in[t4 * 512 + tsub * 128: t4 * 512 + (tsub + 1) * 128,
                                             hg * 512:(hg + 1) * 512],
                                ).then_inc(S2("XL3", tsub % 2), 16)

                @blk.gpsimd
                def _(g):
                    g.wait_ge(S2("ATS", 0), ATS_ALL)
                    g.wait_ge(S2("ATS", 1), ATS_ALL)
                    g.sem_inc(S("DMM"), 2)
                    g.sem_inc(S("SCL"), 4)
                    g.sem_inc(S("TH"), 4)
                    g.sem_inc(S("WOM"), 64)
                    g.sem_inc(S("CMB"), 4)

                    def em_dt(ghg, igs):
                        t4, hg = divmod(ghg, 8)
                        if t4 >= 4:
                            return
                        for ig in igs:
                            cnt = 8 if ig < 10 else 3
                            par = ig % 2
                            g.wait_ge(S("DMM"), ghg * NGI + ig + 1)
                            g.dma_start(
                                out=dtb2[par][:].rearrange("p (c h) -> p c h", h=512)[:, 0:cnt, :],
                                in_=dt_v[:, ig_base(ig):ig_base(ig) + cnt,
                                         hg * 512:(hg + 1) * 512],
                            ).then_inc(S2("DTB", par), 16)

                    def em_wo(ghg):
                        t4, hg = divmod(ghg, 8)
                        for ps in range(2):
                            for ck in range(CK):
                                idx = ps * 32 + ck
                                P = 64 * ghg + idx
                                if P >= 16:
                                    g.wait_ge(S("WOM"), P - 16 + 1 + 64)
                                g.dma_start(
                                    out=wo_buf[P % 16][:],
                                    in_=woT[ck * 128:(ck + 1) * 128,
                                            hg * 512 + ps * 256: hg * 512 + (ps + 1) * 256],
                                ).then_inc(S(f"WS{P % 16}"), 16)

                    def em_hat(t4):
                        if t4 >= 4:
                            return
                        if t4 >= 1:
                            g.wait_ge(S("DMM"), t4 * 8 * NGI + 2)
                        for piece in range(8):
                            c0 = piece * 11
                            cnt = min(11, CI - c0)
                            if piece >= 1:
                                g.wait_ge(S("HRL"), 128 * t4 + 16 * piece)
                            g.dma_start(
                                out=h_res[:].rearrange("p (c t) -> p c t", t=512)[:, c0:c0 + cnt, :],
                                in_=h_v[:, c0:c0 + cnt, t4 * 512:(t4 + 1) * 512],
                            ).then_inc(S("HRL"), 16)
                        g.wait_ge(S("WOM"), 64 * 8 * t4 + 64)
                        for piece in range(4):
                            g.dma_start(
                                out=at4[:].rearrange("p (c t) -> p c t", t=512)[:, piece * 8:(piece + 1) * 8, :],
                                in_=at_v4[:, piece * 8:(piece + 1) * 8, t4 * 512:(t4 + 1) * 512],
                            ).then_inc(S("AT4"), 16)

                    em_hat(0)
                    em_dt(0, range(0, 2))
                    for ghg in range(32):
                        em_dt(ghg, range(2, NGI))
                        em_wo(ghg)
                        if ghg % 8 == 7:
                            em_hat(ghg // 8 + 1)
                        em_dt(ghg + 1, range(0, 2))

                @blk.tensor
                def _(t):
                    R = SR["tensor"]
                    t.wait_ge(S2("ATS", 0), ATS_ALL)
                    t.wait_ge(S2("ATS", 1), ATS_ALL)
                    with fori(nc, SR["tensor"], 0, 4) as t4v:
                        with fori(nc, SR["tensor"], 0, 8) as hg:
                            R.wait(S("SIL"), [(t4v, 32), (hg, 4)], 0)
                            for ig in range(NGI):
                                cnt = 8 if ig < 10 else 3
                                par = ig % 2
                                R.wait(S2("DTB", par), [(t4v, 16 * 48), (hg, 16 * 6)], 16 * (ig // 2 + 1))
                                pneed = min((ig_base(ig) + cnt - 1) // 11, 7)
                                R.wait(S("HRL"), [(t4v, 128)], 16 * (pneed + 1))
                                for j in range(cnt):
                                    cchunk = ig_base(ig) + j
                                    for hc in range(4):
                                        mm = t.matmul(pd[:, hc * 512:(hc + 1) * 512],
                                                      dtb2[par][:, j * 512 + hc * 128: j * 512 + hc * 128 + 128],
                                                      h_res[:, cchunk * 512:(cchunk + 1) * 512],
                                                      start=(ig == 0 and j == 0),
                                                      stop=(ig == NGI - 1 and j == cnt - 1))
                                mm.then_inc(S("DMM"), 1)
                            R.wait(S("AT4"), [(t4v, 64)], 64)
                            for ps in range(2):
                                R.wait(S("TH"), [(t4v, 32), (hg, 4)], 2 * ps + 4)
                                for ck in range(CK):
                                    idx = ps * 32 + ck
                                    R.wait(S(f"WS{idx % 16}"), [(t4v, 16 * 32), (hg, 16 * 4)],
                                           16 * (idx // 16 + 1))
                                    for hx in range(2):
                                        mm = t.matmul(p_wf[:, hx * 512:(hx + 1) * 512],
                                                      wo_buf[idx % 16][:, hx * 128:(hx + 1) * 128],
                                                      at4[:, ck * 512:(ck + 1) * 512],
                                                      start=(ck == 0), stop=(ck == CK - 1))
                                    mm.then_inc(S("WOM"), 1)
                            R.wait(S("TH"), [(t4v, 32), (hg, 4)], 8)
                            R.wait(S("CMB"), [(t4v, 32), (hg, 4)], 4)
                            for hc in range(4):
                                for tsub in range(4):
                                    tr = t.transpose(ptrT[:, tsub * 512 + hc * 128: tsub * 512 + (hc + 1) * 128],
                                                     prodb[:, hc * 512 + tsub * 128: hc * 512 + (tsub + 1) * 128],
                                                     sid[:])
                            tr.then_inc(S("TR4"), 1)

                @blk.vector
                def _(v):
                    for t4 in range(4):
                        for hg in range(8):
                            G = t4 * 8 + hg
                            for hc in range(4):
                                v.wait_ge(S("DMM"), (G + 1) * NGI + 2)
                                if hc == 0:
                                    v.wait_ge(S("SCL"), 4 * G + 4)
                                v.tensor_scalar_mul(stmp[:, hc * 512:(hc + 1) * 512],
                                                    pd[:, hc * 512:(hc + 1) * 512],
                                                    dsc[:, hg * 4 + hc:hg * 4 + hc + 1]).then_inc(S("SIL"), 1)
                            if G >= 1:
                                v.wait_ge(S("TR4"), G)
                            for ps in range(2):
                                v.wait_ge(S("WOM"), 64 * G + ps * 32 + 32 + 64)
                                for hx in range(2):
                                    hc = ps * 2 + hx
                                    v.wait_ge(S("SCL"), 4 * G + hc + 1 + 4)
                                    v.tensor_mul(prodb[:, hc * 512:(hc + 1) * 512],
                                                 p_wf[:, hx * 512:(hx + 1) * 512],
                                                 gst[:, hc * 512:(hc + 1) * 512]).then_inc(S("TH"), 1)
                            v.wait_ge(S("TR4"), G + 1)
                            for tsub in range(4):
                                idx = 4 * G + tsub
                                v.wait_ge(S2("XL3", tsub % 2), 16 * (idx // 2 + 1))
                                if idx >= 2:
                                    v.wait_ge(S2("OUT", tsub % 2), 16 * (idx // 2))
                                v.tensor_add(ch_f[tsub % 2][:],
                                             ptrT[:, tsub * 512:(tsub + 1) * 512],
                                             xb[tsub % 2][:]).then_inc(S("CMB"), 1)

                @blk.scalar
                def _(sc):
                    for t4 in range(4):
                        for hg in range(8):
                            G = t4 * 8 + hg
                            for hc in range(4):
                                sc.wait_ge(S("SIL"), 4 * G + hc + 1)
                                if hc == 0:
                                    sc.wait_ge(S("TH"), 4 * G + 4)
                                sc.activation(gst[:, hc * 512:(hc + 1) * 512],
                                              stmp[:, hc * 512:(hc + 1) * 512],
                                              AF.Silu).then_inc(S("SCL"), 1)
                            for tsub in range(4):
                                sc.wait_ge(S("CMB"), 4 * G + tsub + 5)
                                sc.dma_start(
                                    out=out[t4 * 512 + tsub * 128: t4 * 512 + (tsub + 1) * 128,
                                            hg * 512:(hg + 1) * 512],
                                    in_=ch_f[tsub % 2][:],
                                ).then_inc(S2("OUT", tsub % 2), 16)

        with nc.Block("tail") as blk:
            @blk.sync
            def _(sy):
                if lvl >= 3:
                    sy.wait_ge(S2("OUT", 0), 16 * 64)
                    sy.wait_ge(S2("OUT", 1), 16 * 64)
                elif lvl >= 2:
                    if p2_qts > 0:
                        sy.wait_ge(S2("ATS", 0), 16 * 4 * p2_qts + 16)
                        sy.wait_ge(S2("ATS", 1), 16 * 4 * p2_qts + 16)
                    else:
                        sy.wait_ge(S("VTE"), 32)
                elif lvl >= 1:
                    sy.wait_ge(S2("HST", 0), 16 * 43)
                    sy.wait_ge(S2("HST", 1), 16 * 43)
                    sy.wait_ge(S2("DTS", 0), 16 * 43)
                    sy.wait_ge(S2("DTS", 1), 16 * 43)
                else:
                    sy.wait_ge(S2("QST", 0), 16 * 24)
                    sy.wait_ge(S2("QST", 1), 16 * 24)

    return nc


def _prep(inputs):
    x = inputs["L0_attn_hidden_states"]
    B = x.shape[0]
    wqT = np.ascontiguousarray(np.asarray(inputs["L0_attn_Wq"]).T)
    wkT = np.ascontiguousarray(np.asarray(inputs["L0_attn_Wk"]).T)
    wvT = np.ascontiguousarray(np.asarray(inputs["L0_attn_Wv"]).T)
    woT = np.ascontiguousarray(np.asarray(inputs["L0_attn_Wo"]).T)
    gT = np.ascontiguousarray(
        np.asarray(inputs["L0_ffn_gate_wq"]).T.reshape(H, CI, 128).transpose(1, 0, 2))
    uT = np.ascontiguousarray(
        np.asarray(inputs["L0_ffn_up_wq"]).T.reshape(H, CI, 128).transpose(1, 0, 2))
    dT = np.ascontiguousarray(np.asarray(inputs["L0_ffn_down_wq"]).T).reshape(CI, 128, H)
    gsc = np.ascontiguousarray(np.asarray(inputs["L0_ffn_gate_scale"]).reshape(CI, 128).T)
    usc = np.ascontiguousarray(np.asarray(inputs["L0_ffn_up_scale"]).reshape(CI, 128).T)
    dsc = np.ascontiguousarray(np.asarray(inputs["L0_ffn_down_scale"]).reshape(CK, 128).T)

    inv = (1.0 / (10000.0 ** (np.arange(0, 128, 2, dtype=np.float32) / 128.0))).astype(np.float32)
    pos = np.arange(T, dtype=np.float32)
    freqs = pos[:, None] * inv[None, :]
    emb = np.concatenate([freqs, freqs], axis=1)
    cosT = np.ascontiguousarray(np.cos(emb).astype(np.float32).T)
    sin = np.sin(emb).astype(np.float32)
    s2 = np.ascontiguousarray(np.concatenate([-sin[:, :64], sin[:, 64:]], axis=1).T)
    isq = np.float32(1.0 / np.sqrt(128.0))
    cosQ = np.ascontiguousarray(cosT * isq)
    sinQ = np.ascontiguousarray(s2 * isq)
    ident = np.eye(128, dtype=ml_dtypes.bfloat16)
    q = np.arange(128)
    maskd = np.where(q[None, :] <= q[:, None], 0.0, NEG).astype(np.float32)
    maskb = maskd.astype(ml_dtypes.bfloat16)

    shared = dict(wqT=wqT, wkT=wkT, wvT=wvT, woT=woT, gT=gT, uT=uT, dT=dT,
                  gsc=gsc, usc=usc, dsc=dsc, cosT=cosT, sinN=s2,
                  cosQ=cosQ, sinQ=sinQ, ident=ident, maskd=maskd, maskb=maskb)
    in_maps = []
    for c in range(B):
        m = dict(shared)
        m["x"] = np.ascontiguousarray(np.asarray(x[c], dtype=np.float32))
        in_maps.append(m)
    return in_maps


def kernel(**inputs):
    in_maps = _prep(inputs)
    nc = build()
    res = run_bass_kernel_spmd(nc, in_maps, core_ids=list(range(len(in_maps))))
    return np.stack([np.asarray(r["out"], dtype=np.float32) for r in res.results], axis=0)


# version 08 (final)
# speedup vs baseline: 1.4515x; 1.0147x over previous
import numpy as np
import ml_dtypes
from contextlib import ExitStack

import concourse.bass as bass
import concourse.mybir as mybir
from contextlib import contextmanager
from concourse.bass_primitives import RegisterHandles
from concourse.expressions import RuntimeValue
from concourse.ordered_engine_set import OrderedEngineSet
from concourse.bass_utils import run_bass_kernel_spmd

F32 = mybir.dt.float32
BF16 = mybir.dt.bfloat16
I32 = mybir.dt.int32
AF = mybir.ActivationFunctionType
ALU = mybir.AluOpType
AX = mybir.AxisListType
ds = bass.ds

H = 4096
T = 2048
II = 11008
CK = 32
CI = 86
QT = 16
KVH = 1024
EPS = 1e-5
NEG = -1.0e30
NGI = 12


class Scratch:

    def __init__(self, eng, nrot=4, nslot=8):
        self.eng = eng
        ename = str(eng.engine.value)
        self.rot = [eng.alloc_register(f"sr_{ename}_{i}") for i in range(nrot)]
        self.slot = [eng.alloc_register(f"so_{ename}_{i}") for i in range(nslot)]
        self.tmp = eng.alloc_register(f"st_{ename}")
        self.loops = [eng.alloc_register(f"lp_{ename}_{i}") for i in range(3)]
        self.depth = 0
        self.i = 0
        self.vcache = {}

    def _lin(self, r, terms, const):
        first = True
        for var, coeff in terms:
            if first:
                if coeff == 1:
                    self.eng.reg_mov(r, var)
                else:
                    self.eng.reg_alu(r, var, coeff, ALU.mult)
                first = False
            else:
                if coeff == 1:
                    self.eng.reg_alu(r, r, var, ALU.add)
                else:
                    self.eng.reg_alu(self.tmp, var, coeff, ALU.mult)
                    self.eng.reg_alu(r, r, self.tmp, ALU.add)
        if const:
            self.eng.reg_alu(r, r, const, ALU.add)
        return r

    def wait(self, sem, terms, const=0):
        terms = [(v, c) for (v, c) in terms if c != 0 and v is not None]
        if not terms:
            if const > 0:
                self.eng.wait_ge(sem, const)
            return
        r = self.rot[self.i]
        self.i = (self.i + 1) % len(self.rot)
        self.eng.wait_ge(sem, self._lin(r, terms, const))

    def val(self, slot, terms, const, lo, hi):
        terms = [(v, c) for (v, c) in terms if c != 0 and v is not None]
        if not terms:
            return const
        r = self._lin(self.slot[slot], terms, const)
        return RuntimeValue(RegisterHandles([r]), min_val=lo, max_val=hi)


@contextmanager
def fori(nc, R, start, end):
    eng = R.eng
    r = R.loops[R.depth]
    R.depth += 1
    eng.reg_mov(r, start)
    lid = nc.next_id()
    ls, le = f"fori{lid}_loop", f"fori{lid}_end"
    eng.br(ls)
    engines = OrderedEngineSet([eng.engine])
    with nc.body(ls, valid_engines=engines):
        yield RuntimeValue(RegisterHandles([r]), min_val=start, max_val=end - 1)
        eng.reg_alu(r, r, 1, ALU.add)
        eng.br_cmp(r, end, ls, le, "IS_LT")
    nc.switch_bb(le)
    R.depth -= 1


def build(dbg=(), upto="full", p2_qts=QT, detect_races=True):
    order = ["p1", "p3a", "p2", "p2b", "full"]
    lvl = order.index(upto)
    nc = bass.Bass("TRN2", target_bir_lowering=False,
                   detect_race_conditions=detect_races)

    def scratch_kind(name):
        return "ExternalOutput" if name in dbg else "Internal"

    x_in = nc.dram_tensor("x", [T, H], F32, kind="ExternalInput").ap()
    wqT = nc.dram_tensor("wqT", [H, H], F32, kind="ExternalInput").ap()
    wkT = nc.dram_tensor("wkT", [H, KVH], F32, kind="ExternalInput").ap()
    wvT = nc.dram_tensor("wvT", [H, KVH], F32, kind="ExternalInput").ap()
    woT = nc.dram_tensor("woT", [H, H], F32, kind="ExternalInput").ap()
    gT = nc.dram_tensor("gT", [CI, H, 128], I32, kind="ExternalInput").ap()
    uT = nc.dram_tensor("uT", [CI, H, 128], I32, kind="ExternalInput").ap()
    dT = nc.dram_tensor("dT", [CI, 128, H], I32, kind="ExternalInput").ap()
    gsc_in = nc.dram_tensor("gsc", [128, CI], F32, kind="ExternalInput").ap()
    usc_in = nc.dram_tensor("usc", [128, CI], F32, kind="ExternalInput").ap()
    dsc_in = nc.dram_tensor("dsc", [128, CK], F32, kind="ExternalInput").ap()
    cos_in = nc.dram_tensor("cosT", [128, T], F32, kind="ExternalInput").ap()
    sin_in = nc.dram_tensor("sinN", [128, T], F32, kind="ExternalInput").ap()
    cosq_in = nc.dram_tensor("cosQ", [128, T], F32, kind="ExternalInput").ap()
    sinq_in = nc.dram_tensor("sinQ", [128, T], F32, kind="ExternalInput").ap()
    id_in = nc.dram_tensor("ident", [128, 128], BF16, kind="ExternalInput").ap()
    mk_in = nc.dram_tensor("maskd", [128, 128], F32, kind="ExternalInput").ap()
    mkb_in = nc.dram_tensor("maskb", [128, 128], BF16, kind="ExternalInput").ap()

    q_sc = nc.dram_tensor("q_sc", [CK, 128, T], BF16, kind=scratch_kind("q_sc")).ap()
    k_sc = nc.dram_tensor("k_sc", [8, 128, T], BF16, kind=scratch_kind("k_sc")).ap()
    v_sc = nc.dram_tensor("v_sc", [8, 128, T], BF16, kind=scratch_kind("v_sc")).ap()
    h_sc = nc.dram_tensor("h_sc", [CI, 128, T], BF16, kind=scratch_kind("h_sc")).ap()
    dt_sc = nc.dram_tensor("dt_sc", [CI, 128, H], BF16, kind=scratch_kind("dt_sc")).ap()
    at_sc = nc.dram_tensor("at_sc", [CK, 128, T], BF16, kind=scratch_kind("at_sc")).ap()
    s_sc = nc.dram_tensor("s_sc", [QT, 128, H], F32, kind=scratch_kind("s_sc")).ap()
    xn_dbg = None
    if "xn" in dbg:
        xn_dbg = nc.dram_tensor("xn_dbg", [CK, 128, T], BF16, kind="ExternalOutput").ap()
    out = nc.dram_tensor("out", [T, H], F32, kind="ExternalOutput").ap()

    SR = {
        "sync": Scratch(nc.sync, nslot=4), "tensor": Scratch(nc.tensor, nslot=2),
        "vector": Scratch(nc.vector, nslot=6), "scalar": Scratch(nc.scalar, nslot=2),
        "gpsimd": Scratch(nc.gpsimd, nslot=8),
    }

    with ExitStack() as top:
        E = top.enter_context
        gsc = E(nc.sbuf_tensor("gscs", [128, CI], F32))
        usc = E(nc.sbuf_tensor("uscs", [128, CI], F32))
        gsus = E(nc.sbuf_tensor("gsus", [128, CI], F32))
        dsc = E(nc.sbuf_tensor("dscs", [128, CK], F32))
        sid = E(nc.sbuf_tensor("sid", [128, 128], BF16))
        smask = E(nc.sbuf_tensor("smask", [128, 128], F32))
        smaskb = E(nc.sbuf_tensor("smaskb", [128, 128], BF16))
        epsc = E(nc.sbuf_tensor("epsc", [128, 1], F32))

        sems = {}
        for name in [
            "PRE", "SQ1", "SX1", "VARS", "RSTD", "NMR",
            "NRM", "TRP", "XNE", "VC", "SC",
            "PMM", "PFREE", "QBF",
            "PE3", "UEV", "HS",
            "KVL", "VTR", "VTE",
            "SMM", "MX", "EXP", "PB", "PT", "PTE", "PV", "ATE", "RC",
            "HRL", "DMM", "SIL", "SCL", "CMB",
            "WOM", "TH", "AT4", "TR4",
        ] + [f"WS{j}" for j in range(16)] + [f"DS{j}" for j in range(4)] + [
        ]:
            sems[name] = E(nc.semaphore(name))
        for name in ["X1", "X2", "WL", "QST", "GUL", "DTL", "DTS", "HST",
                     "QL", "DTB", "ATS", "XL3", "OUT"]:
            sems[name + "0"] = E(nc.semaphore(name + "0"))
            sems[name + "1"] = E(nc.semaphore(name + "1"))
        S = lambda n: sems[n]
        S2 = lambda n, p: sems[n + str(p)]

        with ExitStack() as ph:
            e = ph.enter_context
            xn = e(nc.sbuf_tensor("xn", [128, CK * T], BF16))

            with ExitStack() as pa:
                e2 = pa.enter_context
                xt = [e2(nc.sbuf_tensor(f"xt{j}", [128, H], F32)) for j in range(2)]
                junk = e2(nc.sbuf_tensor("junk", [128, 3 * QT], F32))
                sx = e2(nc.sbuf_tensor("sx", [128, QT], F32))
                ssq = e2(nc.sbuf_tensor("ssq", [128, QT], F32))
                stat = e2(nc.sbuf_tensor("stat", [128, 4 * QT], F32))
                xnt = [e2(nc.sbuf_tensor(f"xnt{j}", [128, H], BF16)) for j in range(2)]
                ptr = [e2(nc.psum_tensor(f"ptrA{j}", [128, 1024], BF16)) for j in range(2)]
                mean = stat[:, 0:QT]
                var = stat[:, QT:2 * QT]
                rstd = stat[:, 2 * QT:3 * QT]
                nmr = stat[:, 3 * QT:4 * QT]

                with nc.Block("p1a") as blk:
                    @blk.sync
                    def _(sy):
                        sy.dma_start(out=sid[:], in_=id_in).then_inc(S("PRE"), 16)
                        sy.dma_start(out=smask[:], in_=mk_in).then_inc(S("PRE"), 16)
                        sy.dma_start(out=smaskb[:], in_=mkb_in).then_inc(S("PRE"), 16)
                        sy.dma_start(out=gsc[:], in_=gsc_in).then_inc(S("PRE"), 16)
                        sy.dma_start(out=usc[:], in_=usc_in).then_inc(S("PRE"), 16)
                        sy.dma_start(out=dsc[:], in_=dsc_in).then_inc(S("PRE"), 16)
                        for ti in range(QT):
                            if ti >= 2:
                                sy.wait_ge(S("SQ1"), ti - 1)
                                sy.wait_ge(S("SX1"), ti - 1)
                            sy.dma_start(out=xt[ti % 2][:],
                                         in_=x_in[ti * 128:(ti + 1) * 128, :]).then_inc(S2("X1", ti % 2), 16)
                        for ti in range(QT):
                            sy.wait_ge(S("SQ1"), QT - 1 + ti if ti < 2 else QT)
                            sy.wait_ge(S("SX1"), QT - 1 + ti if ti < 2 else QT)
                            if ti >= 2:
                                sy.wait_ge(S("NRM"), ti - 1)
                            sy.dma_start(out=xt[ti % 2][:],
                                         in_=x_in[ti * 128:(ti + 1) * 128, :]).then_inc(S2("X2", ti % 2), 16)

                    @blk.scalar
                    def _(sc):
                        for ti in range(QT):
                            sc.wait_ge(S2("X1", ti % 2), 16 * (ti // 2 + 1))
                            sc.activation(xnt[ti % 2][:], xt[ti % 2][:], AF.Square,
                                          accum_out=ssq[:, ti:ti + 1]).then_inc(S("SQ1"), 1)
                        sc.wait_ge(S("VARS"), 1)
                        sc.activation(junk[:, 0:QT], var, AF.Ln, bias=epsc[:, 0:1]).then_inc(S("SC"), 1)
                        sc.wait_ge(S("SC"), 1)
                        sc.activation(rstd, junk[:, 0:QT], AF.Exp, scale=-0.5).then_inc(S("RSTD"), 1)
                        for ti in range(QT):
                            sc.wait_ge(S2("X2", ti % 2), 16 * (ti // 2 + 1))
                            if ti == 0:
                                sc.wait_ge(S("NMR"), 1)
                            if ti >= 2:
                                sc.wait_ge(S("TRP"), 4 * (ti - 1))
                            sc.activation(xnt[ti % 2][:], xt[ti % 2][:], AF.Identity,
                                          scale=rstd[:, ti:ti + 1],
                                          bias=nmr[:, ti:ti + 1]).then_inc(S("NRM"), 1)

                    @blk.vector
                    def _(v):
                        v.memset(epsc[:], EPS)
                        for ti in range(QT):
                            v.wait_ge(S2("X1", ti % 2), 16 * (ti // 2 + 1))
                            v.reduce_sum(sx[:, ti:ti + 1], xt[ti % 2][:], axis=AX.X).then_inc(S("SX1"), 1)
                        v.wait_ge(S("SQ1"), QT)
                        v.wait_ge(S("SX1"), QT)
                        v.tensor_scalar_mul(mean, sx[:], 1.0 / H)
                        v.tensor_scalar_mul(var, ssq[:], 1.0 / H).then_inc(S("VC"), 1)
                        v.wait_ge(S("VC"), 1)
                        v.tensor_tensor(junk[:, QT:2 * QT], mean, mean, ALU.mult)
                        v.tensor_scalar_mul(junk[:, 2 * QT:3 * QT], mean, -1.0).then_inc(S("VC"), 1)
                        v.wait_ge(S("VC"), 2)
                        v.tensor_tensor(var, var, junk[:, QT:2 * QT], ALU.subtract).then_inc(S("VARS"), 1)
                        v.wait_ge(S("RSTD"), 1)
                        v.tensor_tensor(nmr, junk[:, 2 * QT:3 * QT], rstd, ALU.mult).then_inc(S("NMR"), 1)
                        for ti in range(QT):
                            for grp in range(4):
                                v.wait_ge(S("TRP"), 4 * ti + grp + 1)
                                v.tensor_copy(
                                    out=xn[:].rearrange("p (c t) -> p c t", t=T)[:, grp * 8:(grp + 1) * 8, ti * 128:(ti + 1) * 128],
                                    in_=ptr[grp % 2][:].rearrange("p (c t) -> p c t", t=128),
                                ).then_inc(S("XNE"), 1)

                    @blk.tensor
                    def _(t):
                        t.wait_ge(S("PRE"), 16 * 6)
                        for ti in range(QT):
                            t.wait_ge(S("NRM"), ti + 1)
                            for grp in range(4):
                                gidx = 4 * ti + grp
                                if gidx >= 2:
                                    t.wait_ge(S("XNE"), gidx - 1)
                                for k in range(8):
                                    c = grp * 8 + k
                                    tr = t.transpose(ptr[grp % 2][:, k * 128:(k + 1) * 128],
                                                     xnt[ti % 2][:, c * 128:(c + 1) * 128], sid[:])
                                tr.then_inc(S("TRP"), 1)

            with ExitStack() as pb:
                e2 = pb.enter_context
                cosT = e2(nc.sbuf_tensor("cosTs", [128, T], F32))
                sinN = e2(nc.sbuf_tensor("sinNs", [128, T], F32))
                cosQ = e2(nc.sbuf_tensor("cosQs", [128, T], F32))
                sinQ = e2(nc.sbuf_tensor("sinQs", [128, T], F32))
                wbuf = [e2(nc.sbuf_tensor(f"wb{j}", [128, H], BF16)) for j in range(2)]
                qtmp = e2(nc.sbuf_tensor("qtmp", [128, T], F32))
                t2 = e2(nc.sbuf_tensor("t2", [128, T], F32))
                qbf = [e2(nc.sbuf_tensor(f"qbf{j}", [128, T], BF16)) for j in range(2)]
                pq = [e2(nc.psum_tensor(f"pq{j}", [128, T], F32)) for j in range(2)]

                sections = [(wqT, CK, q_sc, True), (wkT, 8, k_sc, True), (wvT, 8, v_sc, False)]
                bases = [0, CK, CK + 8]

                def em_wload(g, R, base, cv, par, wsrc, first=False):
                    var, coeff, cst = cv
                    if not first:
                        R.wait(S("PMM"), [(var, coeff)], cst + base - 1)
                    nch = wsrc.shape[1] // 128
                    off = R.val(par, [(var, coeff * 128)], cst * 128, 0, (nch - 1) * 128)
                    g.dma_start(
                        out=wbuf[par][:].rearrange("p (c d) -> p c d", d=128),
                        in_=wsrc.rearrange("(c p) d -> p c d", p=128)[:, :, ds(off, 128)],
                    ).then_inc(S2("WL", par), 16)

                def em_wmm(t, R, base, cv, par, first=False):
                    var, coeff, cst = cv
                    R.wait(S2("WL", par), [(var, 16)], 16 * (base // 2) + 16)
                    if not first:
                        R.wait(S("PFREE"), [(var, coeff)], cst + base - 1)
                    for ck in range(CK):
                        for ns in range(4):
                            mm = t.matmul(
                                pq[par][:, ns * 512:(ns + 1) * 512],
                                wbuf[par][:, ck * 128:(ck + 1) * 128],
                                xn[:, ck * T + ns * 512: ck * T + (ns + 1) * 512],
                                start=(ck == 0), stop=(ck == CK - 1),
                            )
                    mm.then_inc(S("PMM"), 1)

                def em_rope(v, R, base, cv, par, rope, first=False, tabs=None):
                    ct, st = tabs if tabs is not None else (cosT, sinN)
                    var, coeff, cst = cv
                    R.wait(S("PMM"), [(var, coeff)], cst + base + 1)
                    if not first:
                        R.wait(S2("QST", par), [(var, 16)], 16 * (base // 2))
                    if rope:
                        if not (isinstance(cst, int) and var is None and cst + base == 0):
                            R.wait(S("QBF"), [(var, coeff)], cst + base)
                        v.tensor_tensor(qtmp[:], pq[par][:], ct[:], ALU.mult)
                        v.tensor_tensor(t2[0:64, :], pq[par][64:128, :], st[0:64, :], ALU.mult)
                        v.tensor_tensor(t2[64:128, :], pq[par][0:64, :], st[64:128, :],
                                        ALU.mult).then_inc(S("PFREE"), 1)
                        R.wait(S("PFREE"), [(var, coeff)], cst + base + 1)
                        v.tensor_tensor(qbf[par][:], qtmp[:], t2[:], ALU.add).then_inc(S("QBF"), 1)
                    else:
                        v.tensor_copy(out=qbf[par][:], in_=pq[par][:]).then_inc(S("PFREE"), 1)

                def em_store(sc, R, base, cv, par, dst, rope, n):
                    var, coeff, cst = cv
                    R.wait(S("QBF") if rope else S("PFREE"), [(var, coeff)], cst + base + 1)
                    off = R.val(par, [(var, coeff)], cst, 0, n - 1)
                    sc.dma_start(out=dst[ds(off, 1)].rearrange("n p t -> p (n t)"),
                                 in_=qbf[par][:]).then_inc(S2("QST", par), 16)

                with nc.Block("p1b") as blk:
                    @blk.sync
                    def _(sy):
                        sy.wait_ge(S("XNE"), 4 * QT)
                        sy.dma_start(out=cosT[:], in_=cos_in).then_inc(S("PRE"), 16)
                        sy.dma_start(out=sinN[:], in_=sin_in).then_inc(S("PRE"), 16)
                        sy.dma_start(out=cosQ[:], in_=cosq_in).then_inc(S("PRE"), 16)
                        sy.dma_start(out=sinQ[:], in_=sinq_in).then_inc(S("PRE"), 16)

                    @blk.gpsimd
                    def _(g):
                        R = SR["gpsimd"]
                        g.wait_ge(S("XNE"), 4 * QT)
                        em_wload(g, R, 0, (None, 0, 0), 0, wqT, first=True)
                        em_wload(g, R, 0, (None, 0, 1), 1, wqT, first=True)
                        with fori(nc, SR["gpsimd"], 1, CK // 2) as cp:
                            em_wload(g, R, 0, (cp, 2, 0), 0, wqT)
                            em_wload(g, R, 0, (cp, 2, 1), 1, wqT)
                        for si in (1, 2):
                            wsrc, n, _, _ = sections[si]
                            with fori(nc, SR["gpsimd"], 0, n // 2) as cp:
                                em_wload(g, R, bases[si], (cp, 2, 0), 0, wsrc)
                                em_wload(g, R, bases[si], (cp, 2, 1), 1, wsrc)

                    @blk.tensor
                    def _(t):
                        R = SR["tensor"]
                        t.wait_ge(S("XNE"), 4 * QT)
                        em_wmm(t, R, 0, (None, 0, 0), 0, first=True)
                        em_wmm(t, R, 0, (None, 0, 1), 1, first=True)
                        with fori(nc, SR["tensor"], 1, CK // 2) as cp:
                            em_wmm(t, R, 0, (cp, 2, 0), 0)
                            em_wmm(t, R, 0, (cp, 2, 1), 1)
                        for si in (1, 2):
                            n = sections[si][1]
                            with fori(nc, SR["tensor"], 0, n // 2) as cp:
                                em_wmm(t, R, bases[si], (cp, 2, 0), 0)
                                em_wmm(t, R, bases[si], (cp, 2, 1), 1)

                    @blk.vector
                    def _(v):
                        R = SR["vector"]
                        v.wait_ge(S("PRE"), 16 * 10)
                        qt_tabs = (cosQ, sinQ)
                        em_rope(v, R, 0, (None, 0, 0), 0, True, first=True, tabs=qt_tabs)
                        em_rope(v, R, 0, (None, 0, 1), 1, True, first=True, tabs=qt_tabs)
                        with fori(nc, SR["vector"], 1, CK // 2) as cp:
                            em_rope(v, R, 0, (cp, 2, 0), 0, True, tabs=qt_tabs)
                            em_rope(v, R, 0, (cp, 2, 1), 1, True, tabs=qt_tabs)
                        for si in (1, 2):
                            n = sections[si][1]
                            rope = sections[si][3]
                            with fori(nc, SR["vector"], 0, n // 2) as cp:
                                em_rope(v, R, bases[si], (cp, 2, 0), 0, rope)
                                em_rope(v, R, bases[si], (cp, 2, 1), 1, rope)

                    @blk.scalar
                    def _(sc):
                        R = SR["scalar"]
                        em_store(sc, R, 0, (None, 0, 0), 0, q_sc, True, CK)
                        em_store(sc, R, 0, (None, 0, 1), 1, q_sc, True, CK)
                        with fori(nc, SR["scalar"], 1, CK // 2) as cp:
                            em_store(sc, R, 0, (cp, 2, 0), 0, q_sc, True, CK)
                            em_store(sc, R, 0, (cp, 2, 1), 1, q_sc, True, CK)
                        for si in (1, 2):
                            n = sections[si][1]
                            dst = sections[si][2]
                            rope = sections[si][3]
                            with fori(nc, SR["scalar"], 0, n // 2) as cp:
                                em_store(sc, R, bases[si], (cp, 2, 0), 0, dst, rope, n)
                                em_store(sc, R, bases[si], (cp, 2, 1), 1, dst, rope, n)

            with ExitStack() as pc:
              if lvl >= 1:
                e2 = pc.enter_context
                gbuf = [e2(nc.sbuf_tensor(f"gb{j}", [128, H], BF16)) for j in range(2)]
                ubuf = [e2(nc.sbuf_tensor(f"ub{j}", [128, H], BF16)) for j in range(2)]
                uev = e2(nc.sbuf_tensor("uev", [128, T], F32))
                hst = [e2(nc.sbuf_tensor(f"hst{j}", [128, T], BF16)) for j in range(2)]
                dtb = [e2(nc.sbuf_tensor(f"dtb{j}", [128, H], BF16)) for j in range(2)]
                pg = e2(nc.psum_tensor("pg", [128, T], F32))
                pu = e2(nc.psum_tensor("pu", [128, T], F32))

                def em3_load(g, R, cv, par, first=False):
                    var, coeff, cst = cv
                    if not first:
                        R.wait(S("PE3"), [(var, 2 * coeff)], 2 * cst - 2)
                    sb = par * 4
                    off = R.val(sb + 0, [(var, coeff)], cst, 0, CI - 1)
                    g.dma_start(out=gbuf[par][:].rearrange("p (c i) -> p c i", i=128),
                                in_=gT[ds(off, 1)].rearrange("n (c p) i -> p (n c) i", p=128)).then_inc(S2("GUL", par), 16)
                    off = R.val(sb + 1, [(var, coeff)], cst, 0, CI - 1)
                    g.dma_start(out=ubuf[par][:].rearrange("p (c i) -> p c i", i=128),
                                in_=uT[ds(off, 1)].rearrange("n (c p) i -> p (n c) i", p=128)).then_inc(S2("GUL", par), 16)
                    if not first:
                        R.wait(S2("DTS", par), [(var, 16)], 0)
                    off = R.val(sb + 2, [(var, coeff)], cst, 0, CI - 1)
                    g.dma_start(out=dtb[par][:], in_=dT[ds(off, 1)].rearrange("n p h -> p (n h)")).then_inc(S2("DTL", par), 16)

                def em3_mm(t, R, cv, par, first=False):
                    var, coeff, cst = cv
                    R.wait(S2("GUL", par), [(var, 32)], 32)
                    for th in range(2):
                        if not first:
                            R.wait(S("HS"), [(var, 2 * coeff)], 2 * cst + th - 1)
                        for w in range(2):
                            buf = gbuf[par] if w == 0 else ubuf[par]
                            ps = pg if w == 0 else pu
                            for ck in range(CK):
                                for ns in range(2):
                                    mm = t.matmul(
                                        ps[:, th * 1024 + ns * 512: th * 1024 + (ns + 1) * 512],
                                        buf[:, ck * 128:(ck + 1) * 128],
                                        xn[:, ck * T + th * 1024 + ns * 512: ck * T + th * 1024 + (ns + 1) * 512],
                                        start=(ck == 0), stop=(ck == CK - 1),
                                    )
                            if w == 1:
                                mm.then_inc(S("PE3"), 1)

                def em3_uev(sc, R, cv, par):
                    var, coeff, cst = cv
                    for th in range(2):
                        R.wait(S("PE3"), [(var, 2 * coeff)], 2 * cst + th + 1)
                        sc.activation(uev[:, th * 1024:(th + 1) * 1024],
                                      pu[:, th * 1024:(th + 1) * 1024], AF.Copy).then_inc(S("UEV"), 1)

                def em3_h(v, R, cv, par, first=False):
                    var, coeff, cst = cv
                    for th in range(2):
                        R.wait(S("UEV"), [(var, 2 * coeff)], 2 * cst + th + 1)
                        if not first:
                            R.wait(S2("HST", par), [(var, 16)], 0)
                        off = R.val(par, [(var, coeff)], cst, 0, CI - 1)
                        v.scalar_tensor_tensor(
                            hst[par][:, th * 1024:(th + 1) * 1024],
                            pg[:, th * 1024:(th + 1) * 1024],
                            gsus[:, ds(off, 1)],
                            uev[:, th * 1024:(th + 1) * 1024],
                            ALU.mult, ALU.mult,
                        ).then_inc(S("HS"), 1)

                def em3_store(sy, R, cv, par):
                    var, coeff, cst = cv
                    R.wait(S("HS"), [(var, 2 * coeff)], 2 * cst + 2)
                    off = R.val(par * 2, [(var, coeff)], cst, 0, CI - 1)
                    sy.dma_start(out=h_sc[ds(off, 1)].rearrange("n p t -> p (n t)"),
                                 in_=hst[par][:]).then_inc(S2("HST", par), 16)
                    R.wait(S2("DTL", par), [(var, 16)], 16)
                    off = R.val(par * 2 + 1, [(var, coeff)], cst, 0, CI - 1)
                    sy.dma_start(out=dt_sc[ds(off, 1)].rearrange("n p h -> p (n h)"),
                                 in_=dtb[par][:]).then_inc(S2("DTS", par), 16)

                with nc.Block("p3a") as blk:
                    @blk.gpsimd
                    def _(g):
                        R = SR["gpsimd"]
                        g.wait_ge(S2("QST", 0), 16 * 24)
                        g.wait_ge(S2("QST", 1), 16 * 24)
                        em3_load(g, R, (None, 0, 0), 0, first=True)
                        em3_load(g, R, (None, 0, 1), 1, first=True)
                        with fori(nc, SR["gpsimd"], 1, CI // 2) as cp:
                            em3_load(g, R, (cp, 2, 0), 0)
                            em3_load(g, R, (cp, 2, 1), 1)

                    @blk.tensor
                    def _(t):
                        R = SR["tensor"]
                        em3_mm(t, R, (None, 0, 0), 0, first=True)
                        em3_mm(t, R, (None, 0, 1), 1)
                        with fori(nc, SR["tensor"], 1, CI // 2) as cp:
                            em3_mm(t, R, (cp, 2, 0), 0)
                            em3_mm(t, R, (cp, 2, 1), 1)

                    @blk.scalar
                    def _(sc):
                        R = SR["scalar"]
                        em3_uev(sc, R, (None, 0, 0), 0)
                        em3_uev(sc, R, (None, 0, 1), 1)
                        with fori(nc, SR["scalar"], 1, CI // 2) as cp:
                            em3_uev(sc, R, (cp, 2, 0), 0)
                            em3_uev(sc, R, (cp, 2, 1), 1)

                    @blk.vector
                    def _(v):
                        R = SR["vector"]
                        v.wait_ge(S("PRE"), 16 * 6)
                        v.tensor_tensor(gsus[:], gsc[:], usc[:], ALU.mult).then_inc(S("VC"), 1)
                        v.wait_ge(S("VC"), 3)
                        em3_h(v, R, (None, 0, 0), 0, first=True)
                        em3_h(v, R, (None, 0, 1), 1, first=True)
                        with fori(nc, SR["vector"], 1, CI // 2) as cp:
                            em3_h(v, R, (cp, 2, 0), 0)
                            em3_h(v, R, (cp, 2, 1), 1)

                    @blk.sync
                    def _(sy):
                        R = SR["sync"]
                        em3_store(sy, R, (None, 0, 0), 0)
                        em3_store(sy, R, (None, 0, 1), 1)
                        with fori(nc, SR["sync"], 1, CI // 2) as cp:
                            em3_store(sy, R, (cp, 2, 0), 0)
                            em3_store(sy, R, (cp, 2, 1), 1)

            if xn_dbg is not None:
                with nc.Block("dbgxn") as blk0:
                    @blk0.scalar
                    def _(sc):
                        sc.wait_ge(S("XNE"), 4 * QT)
                        for c in range(CK):
                            sc.dma_start(out=xn_dbg[c], in_=xn[:, c * T:(c + 1) * T]).then_inc(S("CMB"), 16)
                        sc.wait_ge(S("CMB"), 16 * CK)

        with ExitStack() as ph:
          if lvl >= 2:
            e = ph.enter_context
            k_res = e(nc.sbuf_tensor("k_res", [128, 8 * T], BF16))
            v_tm = e(nc.sbuf_tensor("v_tm", [128, 16 * 1024], BF16))
            v_fm = e(nc.sbuf_tensor("v_fm", [128, 8 * T], BF16))
            q_res = [e(nc.sbuf_tensor(f"q_res{j}", [128, CK * 128], BF16)) for j in range(2)]
            pe_sb = [e(nc.sbuf_tensor(f"pe_sb{j}", [128, T], F32)) for j in range(2)]
            pb = [e(nc.sbuf_tensor(f"pbh{j}", [128, T], BF16)) for j in range(4)]
            pt_sb = [e(nc.sbuf_tensor(f"pt_sb{j}", [128, 512], BF16)) for j in range(2)]
            at_st = [e(nc.sbuf_tensor(f"at_st{j}", [128, 512], BF16)) for j in range(2)]
            nmb = [e(nc.sbuf_tensor(f"nmb{j}", [128, 1], F32)) for j in range(2)]
            lcol = [e(nc.sbuf_tensor(f"lcol{j}", [128, 1], F32)) for j in range(2)]
            rcol = [e(nc.sbuf_tensor(f"rcol{j}", [128, 1], F32)) for j in range(2)]
            p_s = e(nc.psum_tensor("p_s", [128, T], F32))
            p_o = [e(nc.psum_tensor(f"p_o{j}", [128, 512], F32)) for j in range(2)]
            ptr2 = [e(nc.psum_tensor(f"ptr2{j}", [128, 512], BF16)) for j in range(2)]

            at_v = at_sc.rearrange("c p t -> p c t")

            with nc.Block("p2s") as blk:
                @blk.sync
                def _(sy):
                    sy.wait_ge(S("PE3"), 2 * CI)
                    sy.wait_ge(S("HS"), 2 * CI)
                    sy.wait_ge(S2("HST", 0), 16 * 43)
                    sy.wait_ge(S2("HST", 1), 16 * 43)
                    sy.wait_ge(S2("DTS", 0), 16 * 43)
                    sy.wait_ge(S2("DTS", 1), 16 * 43)
                    sy.wait_ge(S2("QST", 0), 16 * 24)
                    sy.wait_ge(S2("QST", 1), 16 * 24)
                    sy.dma_start(out=k_res[:].rearrange("p (c t) -> p c t", t=T),
                                 in_=k_sc.rearrange("c p t -> p c t")).then_inc(S("KVL"), 16)
                    sy.dma_start(out=v_fm[:].rearrange("p (c t) -> p c t", t=T),
                                 in_=v_sc.rearrange("c p t -> p c t")).then_inc(S("KVL"), 16)

                @blk.vector
                def _(v):
                    v.sem_inc(S("ATE"), 2)
                    v.sem_inc(S2("ATS", 0), 16)
                    v.sem_inc(S2("ATS", 1), 16)
                    v.sem_inc(S("EXP"), 2)
                    v.sem_inc(S("PB"), 2)
                    v.sem_inc(S("PTE"), 2)
                    for grp in range(32):
                        v.wait_ge(S("VTR"), grp + 1)
                        kb, cq = grp // 2, grp % 2
                        v.tensor_copy(out=v_tm[:, kb * 1024 + cq * 512: kb * 1024 + (cq + 1) * 512],
                                      in_=ptr2[grp % 2][:]).then_inc(S("VTE"), 1)

                @blk.tensor
                def _(t):
                    t.wait_ge(S("HS"), 2 * CI)
                    t.wait_ge(S("KVL"), 32)
                    for grp in range(32):
                        kb, cq = grp // 2, grp % 2
                        if grp >= 2:
                            t.wait_ge(S("VTE"), grp - 1)
                        for j in range(4):
                            c = cq * 4 + j
                            tr = t.transpose(ptr2[grp % 2][:, j * 128:(j + 1) * 128],
                                             v_fm[:, c * T + kb * 128: c * T + kb * 128 + 128], sid[:])
                        tr.then_inc(S("VTR"), 1)

            for qt in range(min(QT, p2_qts)):
                klen = (qt + 1) * 128
                nfull, rem = divmod(klen, 512)
                heads_base = 32 * qt
                pt_base = 8 * (qt * (qt + 1) // 2)
                hg_base = 8 * qt

                with nc.Block(f"p2q{qt}") as blk:
                    @blk.sync
                    def _(sy):
                        if qt >= 2:
                            sy.wait_ge(S("SMM"), 32 * (qt - 1))
                        sy.dma_start(out=q_res[qt % 2][:].rearrange("p (c t) -> p c t", t=128),
                                     in_=q_sc.rearrange("c p t -> p c t")[:, :, qt * 128:(qt + 1) * 128]
                                     ).then_inc(S2("QL", qt % 2), 16)

                    @blk.tensor
                    def _(t):
                        for hg in range(8):
                            hgp = hg % 2
                            for j in range(4):
                                g = heads_base + hg * 4 + j
                                sb_off = (g % 2) * 1024 if qt <= 7 else 0
                                t.wait_ge(S("EXP"), (g + 1) if qt <= 7 else (g + 2))
                                if j == 0:
                                    t.wait_ge(S2("QL", qt % 2), 16 * (qt // 2 + 1))
                                    t.wait_ge(S("PB"), g + 2)
                                lhs = q_res[qt % 2][:, (hg * 4 + j) * 128:(hg * 4 + j + 1) * 128]
                                for ns in range(nfull):
                                    mm = t.matmul(p_s[:, sb_off + ns * 512: sb_off + (ns + 1) * 512], lhs,
                                                  k_res[:, hg * T + ns * 512: hg * T + (ns + 1) * 512],
                                                  start=True, stop=(qt * 128 < ns * 512 or qt * 128 >= (ns + 1) * 512))
                                if rem:
                                    mm = t.matmul(p_s[:, sb_off + nfull * 512: sb_off + nfull * 512 + rem], lhs,
                                                  k_res[:, hg * T + nfull * 512: hg * T + nfull * 512 + rem],
                                                  start=True, stop=False)
                                mm = t.matmul(p_s[:, sb_off + qt * 128: sb_off + (qt + 1) * 128], sid[:], smaskb[:],
                                              start=False, stop=True)
                                mm.then_inc(S("SMM"), 1)
                            t.wait_ge(S("PB"), heads_base + hg * 4 + 4 + 2)
                            for kb in range(qt + 1):
                                G = pt_base + hg * (qt + 1) + kb
                                if kb < 2:
                                    t.wait_ge(S("PTE"), pt_base + hg * (qt + 1) + 2)
                                else:
                                    t.wait_ge(S("PTE"), G + 1)
                                for j in range(4):
                                    tr = t.transpose(ptr2[kb % 2][:, j * 128:(j + 1) * 128],
                                                     pb[j][:, kb * 128:(kb + 1) * 128], sid[:])
                                tr.then_inc(S("PT"), 1)
                                if kb >= 1:
                                    t.wait_ge(S("PTE"), G + 2)
                                    if kb == 1:
                                        t.wait_ge(S("ATE"), hg_base + hg + 1)
                                    mm = t.matmul(p_o[hgp][:],
                                                  v_tm[:, (kb - 1) * 1024 + hg * 128: (kb - 1) * 1024 + hg * 128 + 128],
                                                  pt_sb[(kb - 1) % 2][:],
                                                  start=(kb == 1), stop=False)
                            t.wait_ge(S("PTE"), pt_base + (hg + 1) * (qt + 1) + 2)
                            if qt == 0:
                                t.wait_ge(S("ATE"), hg_base + hg + 1)
                            mm = t.matmul(p_o[hgp][:],
                                          v_tm[:, qt * 1024 + hg * 128: qt * 1024 + hg * 128 + 128],
                                          pt_sb[qt % 2][:],
                                          start=(qt == 0), stop=True)
                            mm.then_inc(S("PV"), 1)

                    @blk.vector
                    def _(v):
                        R = SR["vector"]
                        with fori(nc, SR["vector"], 0, 4) as hp:
                            for hgp in range(2):
                                def em_norm(j):
                                    gc = heads_base + hgp * 4 + j
                                    R.wait(S("EXP"), [(hp, 8)], gc + 3)
                                    v.reciprocal(rcol[j % 2][:], lcol[j % 2][:]).then_inc(S("RC"), 1)
                                    R.wait(S("RC"), [(hp, 8)], gc + 1)
                                    v.tensor_scalar_mul(pb[j][:, 0:klen], pe_sb[j % 2][:, 0:klen],
                                                        rcol[j % 2][:]).then_inc(S("PB"), 1)
                                for j in range(4):
                                    gc = heads_base + hgp * 4 + j
                                    sb_off = ((hgp * 4 + j) % 2) * 1024 if qt <= 7 else 0
                                    R.wait(S("SMM"), [(hp, 8)], gc + 1)
                                    R.wait(S("EXP"), [(hp, 8)], gc + 1)
                                    v.reduce_max(nmb[j % 2][:], p_s[:, sb_off:sb_off + klen], axis=AX.X,
                                                 negate=True).then_inc(S("MX"), 2)
                                    if j == 1:
                                        R.wait(S("PT"), [(hp, 2 * (qt + 1))], pt_base + hgp * (qt + 1))
                                    if j >= 1:
                                        em_norm(j - 1)
                                em_norm(3)
                                for kb in range(qt + 1):
                                    R.wait(S("PT"), [(hp, 2 * (qt + 1))], pt_base + hgp * (qt + 1) + kb + 1)
                                    v.tensor_copy(out=pt_sb[kb % 2][:], in_=ptr2[kb % 2][:]).then_inc(S("PTE"), 1)
                                R.wait(S("PV"), [(hp, 2)], hg_base + hgp + 1)
                                R.wait(S2("ATS", hgp), [(hp, 16)], 64 * qt + 16)
                                v.tensor_copy(out=at_st[hgp][:], in_=p_o[hgp][:]).then_inc(S("ATE"), 1)

                    @blk.scalar
                    def _(sc):
                        for hg in range(8):
                            hgp = hg % 2
                            for j in range(4):
                                g = heads_base + hg * 4 + j
                                sb_off = (g % 2) * 1024 if qt <= 7 else 0
                                sc.wait_ge(S("MX"), 2 * g + 2)
                                sc.wait_ge(S("PB"), g + 1)
                                sc.activation(pe_sb[j % 2][:, 0:klen], p_s[:, sb_off:sb_off + klen], AF.Exp,
                                              bias=nmb[j % 2][:],
                                              accum_out=lcol[j % 2][:]).then_inc(S("EXP"), 1)
                            sc.wait_ge(S("ATE"), hg_base + hg + 1 + 2)
                            sc.dma_start(
                                out=at_v[:, hg * 4:(hg + 1) * 4, qt * 128:(qt + 1) * 128],
                                in_=at_st[hgp][:].rearrange("p (j q) -> p j q", q=128),
                            ).then_inc(S2("ATS", hgp), 16)

        with ExitStack() as ph:
          if lvl >= 3:
            e = ph.enter_context
            h_res = e(nc.sbuf_tensor("h_res", [128, CI * 512], BF16))
            dtb2 = [e(nc.sbuf_tensor(f"dtb2{j}", [128, 8 * 512], BF16)) for j in range(4)]
            at4 = e(nc.sbuf_tensor("at4", [128, CK * 512], BF16))
            wo_buf = [e(nc.sbuf_tensor(f"wo{j}", [128, 256], BF16)) for j in range(16)]
            stmp = e(nc.sbuf_tensor("stmp", [128, 2048], F32))
            gst = e(nc.sbuf_tensor("gst", [128, 2048], BF16))
            prodb = e(nc.sbuf_tensor("prodb", [128, 2048], BF16))
            xb = [e(nc.sbuf_tensor(f"xb{j}", [128, 512], F32)) for j in range(2)]
            ch_f = [e(nc.sbuf_tensor(f"ch{j}", [128, 512], F32)) for j in range(2)]
            pd = e(nc.psum_tensor("pd", [128, 2048], F32))
            p_wf = e(nc.psum_tensor("p_wf", [128, 1024], F32))
            ptrT = e(nc.psum_tensor("ptrT", [128, 2048], BF16))

            h_v = h_sc.rearrange("c p t -> p c t")
            dt_v = dt_sc.rearrange("c p h -> p c h")
            at_v4 = at_sc.rearrange("c p t -> p c t")
            ATS_ALL = 16 * 64 + 16

            def ig_base(ig):
                return ig * 8 if ig < 10 else 80 + (ig - 10) * 3

            with nc.Block("p4") as blk:
                @blk.sync
                def _(sy):
                    sy.wait_ge(S2("ATS", 0), ATS_ALL)
                    sy.wait_ge(S2("ATS", 1), ATS_ALL)
                    for t4 in range(4):
                        for hg in range(8):
                            G = t4 * 8 + hg
                            for tsub in range(4):
                                if 4 * G + tsub >= 2:
                                    sy.wait_ge(S("CMB"), 4 * G + tsub + 3)
                                sy.dma_start(
                                    out=xb[tsub % 2][:],
                                    in_=x_in[t4 * 512 + tsub * 128: t4 * 512 + (tsub + 1) * 128,
                                             hg * 512:(hg + 1) * 512],
                                ).then_inc(S2("XL3", tsub % 2), 16)

                @blk.gpsimd
                def _(g):
                    g.wait_ge(S2("ATS", 0), ATS_ALL)
                    g.wait_ge(S2("ATS", 1), ATS_ALL)
                    g.wait_ge(S2("DTS", 0), 16 * 43)
                    g.wait_ge(S2("DTS", 1), 16 * 43)
                    g.sem_inc(S("DMM"), 2)
                    g.sem_inc(S("SCL"), 4)
                    g.sem_inc(S("TH"), 4)
                    g.sem_inc(S("WOM"), 64)
                    g.sem_inc(S("CMB"), 4)

                    def em_dt(ghg, igs):
                        t4, hg = divmod(ghg, 8)
                        if t4 >= 4:
                            return
                        for ig in igs:
                            cnt = 8 if ig < 10 else 3
                            gidx = ghg * NGI + ig
                            if gidx >= 4:
                                g.wait_ge(S("DMM"), gidx - 1)
                            g.dma_start(
                                out=dtb2[gidx % 4][:].rearrange("p (c h) -> p c h", h=512)[:, 0:cnt, :],
                                in_=dt_v[:, ig_base(ig):ig_base(ig) + cnt,
                                         hg * 512:(hg + 1) * 512],
                            ).then_inc(S(f"DS{gidx % 4}"), 16)

                    def em_wo(ghg):
                        t4, hg = divmod(ghg, 8)
                        for ps in range(2):
                            for ck in range(CK):
                                idx = ps * 32 + ck
                                P = 64 * ghg + idx
                                if P >= 16:
                                    g.wait_ge(S("WOM"), P - 16 + 1 + 64)
                                g.dma_start(
                                    out=wo_buf[P % 16][:],
                                    in_=woT[ck * 128:(ck + 1) * 128,
                                            hg * 512 + ps * 256: hg * 512 + (ps + 1) * 256],
                                ).then_inc(S(f"WS{P % 16}"), 16)

                    def em_hat(t4):
                        if t4 >= 4:
                            return
                        if t4 >= 1:
                            g.wait_ge(S("DMM"), t4 * 8 * NGI + 2)
                        for piece in range(8):
                            c0 = piece * 11
                            cnt = min(11, CI - c0)
                            if piece >= 1:
                                g.wait_ge(S("HRL"), 128 * t4 + 16 * piece)
                            g.dma_start(
                                out=h_res[:].rearrange("p (c t) -> p c t", t=512)[:, c0:c0 + cnt, :],
                                in_=h_v[:, c0:c0 + cnt, t4 * 512:(t4 + 1) * 512],
                            ).then_inc(S("HRL"), 16)
                        g.wait_ge(S("WOM"), 64 * 8 * t4 + 64)
                        for piece in range(4):
                            g.dma_start(
                                out=at4[:].rearrange("p (c t) -> p c t", t=512)[:, piece * 8:(piece + 1) * 8, :],
                                in_=at_v4[:, piece * 8:(piece + 1) * 8, t4 * 512:(t4 + 1) * 512],
                            ).then_inc(S("AT4"), 16)

                    em_hat(0)
                    em_dt(0, range(0, 2))
                    for ghg in range(32):
                        em_dt(ghg, range(2, NGI))
                        em_wo(ghg)
                        if ghg % 8 == 7:
                            em_hat(ghg // 8 + 1)
                        em_dt(ghg + 1, range(0, 2))

                @blk.tensor
                def _(t):
                    R = SR["tensor"]
                    t.wait_ge(S2("ATS", 0), ATS_ALL)
                    t.wait_ge(S2("ATS", 1), ATS_ALL)
                    with fori(nc, SR["tensor"], 0, 4) as t4v:
                        with fori(nc, SR["tensor"], 0, 8) as hg:
                            R.wait(S("SIL"), [(t4v, 32), (hg, 4)], 0)
                            for ig in range(NGI):
                                cnt = 8 if ig < 10 else 3
                                R.wait(S(f"DS{ig % 4}"), [(t4v, 16 * 24), (hg, 16 * 3)], 16 * (ig // 4 + 1))
                                pneed = min((ig_base(ig) + cnt - 1) // 11, 7)
                                R.wait(S("HRL"), [(t4v, 128)], 16 * (pneed + 1))
                                for j in range(cnt):
                                    cchunk = ig_base(ig) + j
                                    for hc in range(4):
                                        mm = t.matmul(pd[:, hc * 512:(hc + 1) * 512],
                                                      dtb2[ig % 4][:, j * 512 + hc * 128: j * 512 + hc * 128 + 128],
                                                      h_res[:, cchunk * 512:(cchunk + 1) * 512],
                                                      start=(ig == 0 and j == 0),
                                                      stop=(ig == NGI - 1 and j == cnt - 1))
                                mm.then_inc(S("DMM"), 1)
                            R.wait(S("AT4"), [(t4v, 64)], 64)
                            for ps in range(2):
                                R.wait(S("TH"), [(t4v, 32), (hg, 4)], 2 * ps + 4)
                                for ck in range(CK):
                                    idx = ps * 32 + ck
                                    R.wait(S(f"WS{idx % 16}"), [(t4v, 16 * 32), (hg, 16 * 4)],
                                           16 * (idx // 16 + 1))
                                    for hx in range(2):
                                        mm = t.matmul(p_wf[:, hx * 512:(hx + 1) * 512],
                                                      wo_buf[idx % 16][:, hx * 128:(hx + 1) * 128],
                                                      at4[:, ck * 512:(ck + 1) * 512],
                                                      start=(ck == 0), stop=(ck == CK - 1))
                                    mm.then_inc(S("WOM"), 1)
                            R.wait(S("TH"), [(t4v, 32), (hg, 4)], 8)
                            R.wait(S("CMB"), [(t4v, 32), (hg, 4)], 4)
                            for hc in range(4):
                                for tsub in range(4):
                                    tr = t.transpose(ptrT[:, tsub * 512 + hc * 128: tsub * 512 + (hc + 1) * 128],
                                                     prodb[:, hc * 512 + tsub * 128: hc * 512 + (tsub + 1) * 128],
                                                     sid[:])
                            tr.then_inc(S("TR4"), 1)

                @blk.vector
                def _(v):
                    for t4 in range(4):
                        for hg in range(8):
                            G = t4 * 8 + hg
                            for hc in range(4):
                                v.wait_ge(S("DMM"), (G + 1) * NGI + 2)
                                if hc == 0:
                                    v.wait_ge(S("SCL"), 4 * G + 4)
                                v.tensor_scalar_mul(stmp[:, hc * 512:(hc + 1) * 512],
                                                    pd[:, hc * 512:(hc + 1) * 512],
                                                    dsc[:, hg * 4 + hc:hg * 4 + hc + 1]).then_inc(S("SIL"), 1)
                            if G >= 1:
                                v.wait_ge(S("TR4"), G)
                            for ps in range(2):
                                v.wait_ge(S("WOM"), 64 * G + ps * 32 + 32 + 64)
                                for hx in range(2):
                                    hc = ps * 2 + hx
                                    v.wait_ge(S("SCL"), 4 * G + hc + 1 + 4)
                                    v.tensor_mul(prodb[:, hc * 512:(hc + 1) * 512],
                                                 p_wf[:, hx * 512:(hx + 1) * 512],
                                                 gst[:, hc * 512:(hc + 1) * 512]).then_inc(S("TH"), 1)
                            v.wait_ge(S("TR4"), G + 1)
                            for tsub in range(4):
                                idx = 4 * G + tsub
                                v.wait_ge(S2("XL3", tsub % 2), 16 * (idx // 2 + 1))
                                if idx >= 2:
                                    v.wait_ge(S2("OUT", tsub % 2), 16 * (idx // 2))
                                v.tensor_add(ch_f[tsub % 2][:],
                                             ptrT[:, tsub * 512:(tsub + 1) * 512],
                                             xb[tsub % 2][:]).then_inc(S("CMB"), 1)

                @blk.scalar
                def _(sc):
                    for t4 in range(4):
                        for hg in range(8):
                            G = t4 * 8 + hg
                            for hc in range(4):
                                sc.wait_ge(S("SIL"), 4 * G + hc + 1)
                                if hc == 0:
                                    sc.wait_ge(S("TH"), 4 * G + 4)
                                sc.activation(gst[:, hc * 512:(hc + 1) * 512],
                                              stmp[:, hc * 512:(hc + 1) * 512],
                                              AF.Silu).then_inc(S("SCL"), 1)
                            for tsub in range(4):
                                sc.wait_ge(S("CMB"), 4 * G + tsub + 5)
                                sc.dma_start(
                                    out=out[t4 * 512 + tsub * 128: t4 * 512 + (tsub + 1) * 128,
                                            hg * 512:(hg + 1) * 512],
                                    in_=ch_f[tsub % 2][:],
                                ).then_inc(S2("OUT", tsub % 2), 16)

        with nc.Block("tail") as blk:
            @blk.sync
            def _(sy):
                if lvl >= 3:
                    sy.wait_ge(S2("OUT", 0), 16 * 64)
                    sy.wait_ge(S2("OUT", 1), 16 * 64)
                elif lvl >= 2:
                    if p2_qts > 0:
                        sy.wait_ge(S2("ATS", 0), 16 * 4 * p2_qts + 16)
                        sy.wait_ge(S2("ATS", 1), 16 * 4 * p2_qts + 16)
                    else:
                        sy.wait_ge(S("VTE"), 32)
                elif lvl >= 1:
                    sy.wait_ge(S2("HST", 0), 16 * 43)
                    sy.wait_ge(S2("HST", 1), 16 * 43)
                    sy.wait_ge(S2("DTS", 0), 16 * 43)
                    sy.wait_ge(S2("DTS", 1), 16 * 43)
                else:
                    sy.wait_ge(S2("QST", 0), 16 * 24)
                    sy.wait_ge(S2("QST", 1), 16 * 24)

    return nc


def _prep(inputs):
    x = inputs["L0_attn_hidden_states"]
    B = x.shape[0]
    wqT = np.ascontiguousarray(np.asarray(inputs["L0_attn_Wq"]).T)
    wkT = np.ascontiguousarray(np.asarray(inputs["L0_attn_Wk"]).T)
    wvT = np.ascontiguousarray(np.asarray(inputs["L0_attn_Wv"]).T)
    woT = np.ascontiguousarray(np.asarray(inputs["L0_attn_Wo"]).T)
    gT = np.ascontiguousarray(
        np.asarray(inputs["L0_ffn_gate_wq"]).T.reshape(H, CI, 128).transpose(1, 0, 2))
    uT = np.ascontiguousarray(
        np.asarray(inputs["L0_ffn_up_wq"]).T.reshape(H, CI, 128).transpose(1, 0, 2))
    dT = np.ascontiguousarray(np.asarray(inputs["L0_ffn_down_wq"]).T).reshape(CI, 128, H)
    gsc = np.ascontiguousarray(np.asarray(inputs["L0_ffn_gate_scale"]).reshape(CI, 128).T)
    usc = np.ascontiguousarray(np.asarray(inputs["L0_ffn_up_scale"]).reshape(CI, 128).T)
    dsc = np.ascontiguousarray(np.asarray(inputs["L0_ffn_down_scale"]).reshape(CK, 128).T)

    inv = (1.0 / (10000.0 ** (np.arange(0, 128, 2, dtype=np.float32) / 128.0))).astype(np.float32)
    pos = np.arange(T, dtype=np.float32)
    freqs = pos[:, None] * inv[None, :]
    emb = np.concatenate([freqs, freqs], axis=1)
    cosT = np.ascontiguousarray(np.cos(emb).astype(np.float32).T)
    sin = np.sin(emb).astype(np.float32)
    s2 = np.ascontiguousarray(np.concatenate([-sin[:, :64], sin[:, 64:]], axis=1).T)
    isq = np.float32(1.0 / np.sqrt(128.0))
    cosQ = np.ascontiguousarray(cosT * isq)
    sinQ = np.ascontiguousarray(s2 * isq)
    ident = np.eye(128, dtype=ml_dtypes.bfloat16)
    q = np.arange(128)
    maskd = np.where(q[None, :] <= q[:, None], 0.0, NEG).astype(np.float32)
    maskb = maskd.astype(ml_dtypes.bfloat16)

    shared = dict(wqT=wqT, wkT=wkT, wvT=wvT, woT=woT, gT=gT, uT=uT, dT=dT,
                  gsc=gsc, usc=usc, dsc=dsc, cosT=cosT, sinN=s2,
                  cosQ=cosQ, sinQ=sinQ, ident=ident, maskd=maskd, maskb=maskb)
    in_maps = []
    for c in range(B):
        m = dict(shared)
        m["x"] = np.ascontiguousarray(np.asarray(x[c], dtype=np.float32))
        in_maps.append(m)
    return in_maps


def kernel(**inputs):
    in_maps = _prep(inputs)
    nc = build()
    res = run_bass_kernel_spmd(nc, in_maps, core_ids=list(range(len(in_maps))))
    return np.stack([np.asarray(r["out"], dtype=np.float32) for r in res.results], axis=0)
